# version 11
# baseline (speedup 1.0000x reference)
import math
import numpy as np
from contextlib import ExitStack

import concourse.bass as bass
import concourse.mybir as mybir
from concourse.bass_utils import run_bass_kernel_spmd

F32 = mybir.dt.float32
BF16 = mybir.dt.bfloat16
AF = mybir.ActivationFunctionType
AX = mybir.AxisListType
OP = mybir.AluOpType

NEG = -1.0e30


class _Sem:
    __slots__ = ("name", "v")

    def __init__(self, name):
        self.name = name
        self.v = 0


class _T:
    __slots__ = ("ready", "users")

    def __init__(self):
        self.ready = []
        self.users = []


class _Prog:
    def __init__(self):
        self.ops = []
        self.sems = []
        self.pw = []
        self.final_waits = []

    def sem(self, name):
        s = _Sem(name)
        self.sems.append(s)
        return s

    def barrier(self, exclude=()):
        self.pw = [(s, s.v) for s in self.sems if s.v > 0 and s.name not in exclude]

    def group(self, eng, fns, reads=(), writes=(), inc=None, amt=1,
              extra_waits=(), accum=False):
        ws = list(extra_waits) + list(self.pw)
        for t in reads:
            ws += t.ready
        for t in writes:
            ws += t.ready + t.users
        wd = {}
        for (s, v) in ws:
            assert v <= s.v, f"wait on future {s.name}: {v} > {s.v}"
            if wd.get(s, 0) < v:
                wd[s] = v
        wt = tuple((s.name, v) for s, v in wd.items())
        assert inc is not None
        inc.v += amt
        iv = (inc.name, amt)
        for i, fn in enumerate(fns):
            self.ops.append((eng, fn,
                             wt if i == 0 else (),
                             iv if i == len(fns) - 1 else None))
        ev = (inc, inc.v)
        for t in reads:
            t.users.append(ev)
        for t in writes:
            if accum:
                t.ready.append(ev)
            else:
                t.ready = [ev]
                t.users = []
        return inc.v


def _emit(nc, prog, hsem):
    engines = ("sync", "scalar", "vector", "tensor", "gpsimd")
    per = {e: [] for e in engines}
    for o in prog.ops:
        per[o[0]].append(o)
    with nc.Block() as block:
        for name in engines:
            lst = per[name]
            fin = prog.final_waits if name == "gpsimd" else []
            if not lst and not fin:
                continue

            def make(lst=lst, fin=fin):
                def run(e):
                    lw = {}
                    for (_, fn, ws, iv) in lst:
                        for (sn, v) in ws:
                            if lw.get(sn, 0) < v:
                                e.wait_ge(hsem[sn], v)
                                lw[sn] = v
                        ins = fn(e)
                        if iv is not None:
                            ins.then_inc(hsem[iv[0]], iv[1])
                    for (sn, v) in fin:
                        e.wait_ge(hsem[sn], v)
                return run

            getattr(block, name)(make())


def build(cfg):
    import os as _os0
    S = cfg["S"]; D = cfg["D"]; F = cfg["F"]; HPC = cfg["HPC"]
    G = 4
    OT = S // G
    TB = cfg["TB"]; TQ = cfg["TQ"]
    IC = D // 128
    FH = F // 128
    NTB = S // TB
    NSUB = TB // 128
    NQT = S // 128
    HD = HPC * 128
    F2 = F // 2
    INVSQ = 1.0 / math.sqrt(128.0)
    assert NQT % 2 == 0 and OT % TQ == 0 and TB % 128 == 0 and S % TB == 0
    assert TQ % 128 == 0 and TB <= 512 and TQ <= 512 and IC % 4 == 0

    nc = bass.Bass("TRN2", target_bir_lowering=False)

    xb = nc.dram_tensor("xb", [S, D], F32, kind="ExternalInput").ap()
    xown = nc.dram_tensor("xown", [OT, D], F32, kind="ExternalInput").ap()
    wq = nc.dram_tensor("wq", [HPC, 128, IC * 128], F32, kind="ExternalInput").ap()
    wk = nc.dram_tensor("wk", [HPC, 128, IC * 128], F32, kind="ExternalInput").ap()
    wv = nc.dram_tensor("wv", [HPC, 128, IC * 128], F32, kind="ExternalInput").ap()
    wo = nc.dram_tensor("wo", [HPC, 128, D], F32, kind="ExternalInput").ap()
    w1 = nc.dram_tensor("w1", [FH, 128, IC * 128], F32, kind="ExternalInput").ap()
    w2 = nc.dram_tensor("w2", [FH, 128, D], F32, kind="ExternalInput").ap()
    bqs = nc.dram_tensor("bqs", [128, HPC], F32, kind="ExternalInput").ap()
    bks = nc.dram_tensor("bks", [128, HPC], F32, kind="ExternalInput").ap()
    bvs = nc.dram_tensor("bvs", [128, HPC], F32, kind="ExternalInput").ap()
    bos = nc.dram_tensor("bos", [128, D], F32, kind="ExternalInput").ap()
    b2bc = nc.dram_tensor("b2bc", [128, D], F32, kind="ExternalInput").ap()
    b1s = nc.dram_tensor("b1s", [128, FH], F32, kind="ExternalInput").ap()
    b2s = nc.dram_tensor("b2s", [128, IC], F32, kind="ExternalInput").ap()
    identd = nc.dram_tensor("identd", [128, 128], F32, kind="ExternalInput").ap()
    maskd = nc.dram_tensor("maskd", [128, 128], F32, kind="ExternalInput").ap()
    outd = nc.dram_tensor("out", [OT, D], F32, kind="ExternalOutput").ap()

    qsd = nc.dram_tensor("qsd", [2, HPC, 128, S], BF16).ap()
    ksd = nc.dram_tensor("ksd", [2, HPC, 128, S], BF16).ap()
    vsd = nc.dram_tensor("vsd", [2, HPC, 128, S], BF16).ap()
    aod = nc.dram_tensor("aod", [2, HPC, 128, S], BF16).ap()
    w1td = nc.dram_tensor("w1td", [2, FH, 128, IC * 128], BF16).ap()
    w2td = nc.dram_tensor("w2td", [2, FH, 128, D], BF16).ap()
    cc_in = nc.dram_tensor("cc_in", [G, OT, D], F32)
    cc_out = nc.dram_tensor("cc_out", [OT, D], F32)

    p = _Prog()
    LD = p.sem("LD"); ST = p.sem("ST"); OU = p.sem("OU")
    MM = p.sem("MM"); VE = p.sem("VE"); SE = p.sem("SE"); CC = p.sem("CC")

    gctx = ExitStack()

    def mksb(ctx, name, shape, dt=F32):
        return ctx.enter_context(nc.sbuf_tensor(name, shape, dt)), _T()

    ident, tI = mksb(gctx, "ident", [128, 128])
    maskt, tMk = mksb(gctx, "maskt", [128, 128])
    epst, tEp = mksb(gctx, "epst", [128, 1])
    bq_sb, tBq = mksb(gctx, "bq_sb", [128, HPC])
    bk_sb, tBk = mksb(gctx, "bk_sb", [128, HPC])
    bv_sb, tBv = mksb(gctx, "bv_sb", [128, HPC])
    bo_sb, tBo = mksb(gctx, "bo_sb", [128, D])
    b2b_sb, tB2b = mksb(gctx, "b2b_sb", [128, D])
    b1_sb, tB1 = mksb(gctx, "b1_sb", [128, FH])
    b2_sb, tB2 = mksb(gctx, "b2_sb", [128, IC])

    pbank = [gctx.enter_context(nc.psum_tensor(f"pb{i}", [128, 512], F32))
             for i in range(8)]
    tPB = [_T() for _ in range(8)]

    def dma(eng, sem, out_ap, in_ap, reads=(), writes=(), extra=(), tr=False):
        if tr:
            fn = lambda e, o=out_ap, i=in_ap: e.dma_start(out=o, in_=i, transpose=True)
        else:
            fn = lambda e, o=out_ap, i=in_ap: e.dma_start(out=o, in_=i)
        return p.group(eng, [fn], reads=reads, writes=writes, inc=sem, amt=16,
                       extra_waits=extra)

    for (t_, sbuf_, dram_) in ((tI, ident, identd), (tMk, maskt, maskd),
                               (tBq, bq_sb, bqs), (tBk, bk_sb, bks),
                               (tBv, bv_sb, bvs), (tBo, bo_sb, bos),
                               (tB2b, b2b_sb, b2bc),
                               (tB1, b1_sb, b1s), (tB2, b2_sb, b2s)):
        dma("sync", LD, sbuf_[:], dram_, writes=(t_,))
    p.group("vector", [lambda e: e.memset(epst[:], 1e-5)], writes=(tEp,), inc=VE)

    def transposes(bank_i, srcs, reads):
        fns = []
        for k, s_ in enumerate(srcs):
            fns.append(lambda e, b=pbank[bank_i][:, k * 128:(k + 1) * 128],
                       s=s_, i=ident[:]: e.transpose(b, s, i))
        p.group("tensor", fns, reads=tuple(reads) + (tI,), writes=(tPB[bank_i],),
                inc=MM)
        return tPB[bank_i]

    def split_psum(bank_i, cols, hi_ap, lo_ap, hi_t, lo_t, accum=False,
                   three_d=False):
        if three_d:
            src = pbank[bank_i][:].rearrange("p (x c) -> p x c", c=128)[:, 0:cols // 128, :]
        else:
            src = pbank[bank_i][:, 0:cols]
        tb = tPB[bank_i]
        p.group("scalar", [lambda e, o=hi_ap, i=src: e.activation(
            o, i, AF.Identity, bias=0.0, scale=1.0)], reads=(tb,), writes=(hi_t,),
            inc=SE, accum=accum)
        p.group("vector", [lambda e, o=lo_ap, i=src, h=hi_ap: e.tensor_tensor(
            o, i, h, OP.subtract)], reads=(tb, hi_t), writes=(lo_t,), inc=VE,
            accum=accum)

    def split_sbuf(src_ap, src_t, hi_ap, lo_ap, hi_t, lo_t, accum=False):
        p.group("scalar", [lambda e, o=hi_ap, i=src_ap: e.activation(
            o, i, AF.Identity, bias=0.0, scale=1.0)], reads=(src_t,), writes=(hi_t,),
            inc=SE, accum=accum)
        p.group("vector", [lambda e, o=lo_ap, i=src_ap, h=hi_ap: e.tensor_tensor(
            o, i, h, OP.subtract)], reads=(src_t, hi_t), writes=(lo_t,), inc=VE,
            accum=accum)

    def mm3(bank_i, cols, pairs, readsets, extra=()):
        fns = []
        n = 0
        total = len(pairs) * 3
        for (lh, ll, rh, rl) in pairs:
            for (a_, b_) in ((lh, rh), (lh, rl), (ll, rh)):
                n += 1
                fns.append(lambda e, o=pbank[bank_i][:, 0:cols], x=a_, y=b_,
                           s=(n == 1), t=(n == total): e.matmul(o, x, y,
                                                               start=s, stop=t))
        p.group("tensor", fns, reads=tuple(readsets), writes=(tPB[bank_i],),
                inc=MM, extra_waits=extra)
        return tPB[bank_i]

    def ln_pipeline(ctx, tag, scr_pair=None):
        if scr_pair is None:
            scr, tScr = mksb(ctx, f"{tag}_scr", [128, D])
        else:
            scr, tScr = scr_pair
        s1, tS1 = mksb(ctx, f"{tag}_s1", [128, 1])
        s2, tS2 = mksb(ctx, f"{tag}_s2", [128, 1])
        mean, tMean = mksb(ctx, f"{tag}_mean", [128, 1])
        nmean, tNmean = mksb(ctx, f"{tag}_nmean", [128, 1])
        std, tStd = mksb(ctx, f"{tag}_std", [128, 1])
        rstd, tRstd = mksb(ctx, f"{tag}_rstd", [128, 1])

        def ln(x_ap, x_t, out_ap, out_t):
            p.group("vector", [lambda e: e.reduce_sum(s1[:], x_ap, axis=AX.X)],
                    reads=(x_t,), writes=(tS1,), inc=VE)
            p.group("scalar", [lambda e: e.activation(nmean[:], s1[:], AF.Identity,
                    bias=0.0, scale=-1.0 / D)], reads=(tS1,), writes=(tNmean,),
                    inc=SE)
            p.group("scalar", [lambda e: e.activation(mean[:], s1[:], AF.Identity,
                    bias=0.0, scale=1.0 / D)], reads=(tS1,), writes=(tMean,), inc=SE)
            p.group("scalar", [lambda e: e.activation(scr[:], x_ap, AF.Square,
                    bias=nmean[:, 0:1], scale=1.0, accum_out=s2[:])],
                    reads=(x_t, tNmean), writes=(tScr, tS2), inc=SE)
            p.group("scalar", [lambda e: e.activation(std[:], s2[:], AF.Sqrt,
                    bias=epst[:, 0:1], scale=1.0 / D)], reads=(tS2, tEp),
                    writes=(tStd,), inc=SE)
            p.group("vector", [lambda e: e.reciprocal(rstd[:], std[:])],
                    reads=(tStd,), writes=(tRstd,), inc=VE)
            p.group("vector", [lambda e: e.tensor_scalar(out_ap, x_ap,
                    mean[:, 0:1], rstd[:, 0:1], OP.subtract, OP.mult)],
                    reads=(x_t, tMean, tRstd), writes=(out_t,), inc=VE)
        return ln

    ctxA = ExitStack()
    wT_hi = []; wT_lo = []; tWThi = []; tWTlo = []
    for t in range(3):
        h_, th_ = mksb(ctxA, f"wTh{t}", [128, IC * HPC * 128], BF16)
        l_, tl_ = mksb(ctxA, f"wTl{t}", [128, IC * HPC * 128], BF16)
        wT_hi.append(h_); wT_lo.append(l_); tWThi.append(th_); tWTlo.append(tl_)
    wn_sb = []; tWn = []
    for i in range(2):
        b_, t_ = mksb(ctxA, f"wn{i}", [128, D])
        wn_sb.append(b_); tWn.append(t_)

    wsrc = [wq, wk, wv]
    for t in range(3):
        for oc in range(HPC):
            buf = (t * HPC + oc) % 2
            dma("sync", LD, wn_sb[buf][:, 0:IC * 128], wsrc[t][oc], writes=(tWn[buf],))
            base = oc * IC * 128
            split_sbuf(wn_sb[buf][:, 0:IC * 128], tWn[buf],
                       wT_hi[t][:, base:base + IC * 128],
                       wT_lo[t][:, base:base + IC * 128], tWThi[t], tWTlo[t],
                       accum=True)

    x_sub = []; tXs = []
    xn_tm = []; tXn = []
    for i in range(2):
        a_, ta_ = mksb(ctxA, f"xsub{i}", [128, D]); x_sub.append(a_); tXs.append(ta_)
        b_, tb_ = mksb(ctxA, f"xntm{i}", [128, D]); xn_tm.append(b_); tXn.append(tb_)
    xnf_hi, tXfh = mksb(ctxA, "xnf_hi", [128, IC * TB], BF16)
    xnf_lo, tXfl = mksb(ctxA, "xnf_lo", [128, IC * TB], BF16)
    qf_sb, tQf = mksb(ctxA, "qf", [128, TB])
    stg_hi = []; stg_lo = []; tSgh = []; tSgl = []
    for i in range(2):
        a_, ta_ = mksb(ctxA, f"stgh{i}", [128, TB], BF16); stg_hi.append(a_); tSgh.append(ta_)
        b_, tb_ = mksb(ctxA, f"stgl{i}", [128, TB], BF16); stg_lo.append(b_); tSgl.append(tb_)
    lnA = ln_pipeline(ctxA, "lnA")

    dsts = [qsd, ksd, vsd]
    biases = [(bq_sb, tBq), (bk_sb, tBk), (bv_sb, tBv)]
    tQS = _T(); tKS = _T(); tVS = _T()
    sc_trk = [tQS, tKS, tVS]
    sbuf_stage_ctr = 0
    for tb in range(NTB):
        for a in range(NSUB):
            g = tb * NSUB + a
            buf = g % 2
            dma("sync", LD, x_sub[buf][:], xb[g * 128:(g + 1) * 128, :],
                writes=(tXs[buf],))
            lnA(x_sub[buf][:], tXs[buf], xn_tm[buf][:], tXn[buf])
            xnr_hi = xnf_hi[:].rearrange("q (i c) -> q i c", c=TB)
            xnr_lo = xnf_lo[:].rearrange("q (i c) -> q i c", c=TB)
            for icg in range(IC // 4):
                bk_i = 6 + (icg % 2)
                transposes(bk_i, [xn_tm[buf][:, (icg * 4 + k) * 128:
                                             (icg * 4 + k + 1) * 128]
                                  for k in range(4)], (tXn[buf],))
                hi_ap = xnr_hi[:, icg * 4:(icg + 1) * 4, a * 128:(a + 1) * 128]
                lo_ap = xnr_lo[:, icg * 4:(icg + 1) * 4, a * 128:(a + 1) * 128]
                split_psum(bk_i, 512, hi_ap, lo_ap, tXfh, tXfl, accum=True,
                           three_d=True)
        for t in range(3):
            for oc in range(HPC):
                bk_i = (t * HPC + oc) % 2
                pairs = []
                for ic in range(IC):
                    wb = (oc * IC + ic) * 128
                    pairs.append((wT_hi[t][:, wb:wb + 128],
                                  wT_lo[t][:, wb:wb + 128],
                                  xnf_hi[:, ic * TB:(ic + 1) * TB],
                                  xnf_lo[:, ic * TB:(ic + 1) * TB]))
                mm3(bk_i, TB, pairs, (tWThi[t], tWTlo[t], tXfh, tXfl))
                bt, btt = biases[t]
                p.group("scalar", [lambda e, o=qf_sb[:], i=pbank[bk_i][:, 0:TB],
                        b=bt[:, oc:oc + 1]: e.activation(o, i, AF.Identity, bias=b,
                        scale=1.0)], reads=(tPB[bk_i], btt), writes=(tQf,), inc=SE)
                sbuf_stage_ctr += 1
                sbf = sbuf_stage_ctr % 2
                split_sbuf(qf_sb[:], tQf, stg_hi[sbf][:], stg_lo[sbf][:],
                           tSgh[sbf], tSgl[sbf])
                dma("gpsimd", ST, dsts[t][0, oc, :, tb * TB:(tb + 1) * TB],
                    stg_hi[sbf][:], reads=(tSgh[sbf],))
                dma("gpsimd", ST, dsts[t][1, oc, :, tb * TB:(tb + 1) * TB],
                    stg_lo[sbf][:], reads=(tSgl[sbf],))
                sc_trk[t].ready.append((ST, ST.v))


    p.barrier()
    ctxA.close()

    ctxB = ExitStack()
    kx_hi, tKxh = mksb(ctxB, "kx_hi", [128, S], BF16)
    kx_lo, tKxl = mksb(ctxB, "kx_lo", [128, S], BF16)
    qx_hi, tQxh = mksb(ctxB, "qx_hi", [128, S], BF16)
    qx_lo, tQxl = mksb(ctxB, "qx_lo", [128, S], BF16)
    vt_hi, tVth = mksb(ctxB, "vt_hi", [128, S], BF16)
    vt_lo, tVtl = mksb(ctxB, "vt_lo", [128, S], BF16)
    LMAX = [NQT // 2 * 128, S, S]
    sc_sb = []; tSc = []; p_sb = []; tP = []
    pt_hi = []; pt_lo = []; tPth = []; tPtl = []
    nm = []; tNm = []; nms = []; tNms = []
    ssum = []; tSs = []; lns_ = []; tLns = []; bb2 = []; tBb2 = []
    aoh = []; aol = []; tAoh = []; tAol = []
    cm = []; tCm = []; rcp = []; tRcp = []
    for sl in range(3):
        a_, ta_ = mksb(ctxB, f"scs{sl}", [128, LMAX[sl]]); sc_sb.append(a_); tSc.append(ta_)
    for sl in range(2):
        b_, tb_ = mksb(ctxB, f"psb{sl}", [128, LMAX[sl]]); p_sb.append(b_); tP.append(tb_)
        c_, tc_ = mksb(ctxB, f"pth{sl}", [128, LMAX[sl]], BF16); pt_hi.append(c_); tPth.append(tc_)
        d_, td_ = mksb(ctxB, f"ptl{sl}", [128, LMAX[sl]], BF16); pt_lo.append(d_); tPtl.append(td_)
        for si, (lst, tl, shape) in enumerate(((nm, tNm, [128, 1]),
                                               (nms, tNms, [128, 1]),
                                               (ssum, tSs, [128, 1]),
                                               (lns_, tLns, [128, 1]),
                                               (bb2, tBb2, [128, 1]))):
            x_, tx_ = mksb(ctxB, f"sm{si}_{sl}", shape)
            lst.append(x_); tl.append(tx_)
        e_, te_ = mksb(ctxB, f"aoh{sl}", [128, 128], BF16); aoh.append(e_); tAoh.append(te_)
        f_, tf_ = mksb(ctxB, f"aol{sl}", [128, 128], BF16); aol.append(f_); tAol.append(tf_)
        g_, tg_ = mksb(ctxB, f"cm{sl}", [128, 8]); cm.append(g_); tCm.append(tg_)
        h2_, th2_ = mksb(ctxB, f"rcp{sl}", [128, 1]); rcp.append(h2_); tRcp.append(th2_)

    wpn = []; tWpn = []
    for i in range(2):
        a_, ta_ = mksb(ctxB, f"wpn{i}", [128, 2048])
        wpn.append(a_); tWpn.append(ta_)
    wps_hi = []; wps_lo = []; tWpsh = []; tWpsl = []
    for i in range(2):
        a_, ta_ = mksb(ctxB, f"wpsh{i}", [128, 2048], BF16); wps_hi.append(a_); tWpsh.append(ta_)
        b_, tb_ = mksb(ctxB, f"wpsl{i}", [128, 2048], BF16); wps_lo.append(b_); tWpsl.append(tb_)
    tW1T = _T(); tW2T = _T()
    W1CW = min(IC * 128, 2048)
    W2CW = min(D, 2048)
    prep_units = []
    for hc in range(FH):
        for qv in range(IC * 128 // W1CW):
            prep_units.append(("w1", hc, qv))
    for hc in range(FH):
        for qv in range(D // W2CW):
            prep_units.append(("w2", hc, qv))
    prep_pos = [0]

    def emit_prep_unit():
        if prep_pos[0] >= len(prep_units):
            return
        kind, a0, qv = prep_units[prep_pos[0]]
        buf = prep_pos[0] % 2
        prep_pos[0] += 1
        if kind == "w1":
            cw = W1CW
            src_ap = w1[a0, :, qv * cw:(qv + 1) * cw]
            dst_hi = w1td[0, a0, :, qv * cw:(qv + 1) * cw]
            dst_lo = w1td[1, a0, :, qv * cw:(qv + 1) * cw]
            trk = tW1T
        else:
            cw = W2CW
            src_ap = w2[a0, :, qv * cw:(qv + 1) * cw]
            dst_hi = w2td[0, a0, :, qv * cw:(qv + 1) * cw]
            dst_lo = w2td[1, a0, :, qv * cw:(qv + 1) * cw]
            trk = tW2T
        dma("sync", LD, wpn[buf][:, 0:cw], src_ap, writes=(tWpn[buf],))
        split_sbuf(wpn[buf][:, 0:cw], tWpn[buf], wps_hi[buf][:, 0:cw],
                   wps_lo[buf][:, 0:cw], tWpsh[buf], tWpsl[buf])
        dma("gpsimd", ST, dst_hi, wps_hi[buf][:, 0:cw], reads=(tWpsh[buf],))
        dma("gpsimd", ST, dst_lo, wps_lo[buf][:, 0:cw], reads=(tWpsl[buf],))
        trk.ready.append((ST, ST.v))

    n_pairs_total = HPC * (NQT // 2)
    prep_per_pair = (len(prep_units) + n_pairs_total - 1) // n_pairs_total

    tAOD = _T()
    n_pairs_total = HPC * (NQT // 2)
    prep_per_pair = (len(prep_units) + n_pairs_total - 1) // n_pairs_total

    for h in range(HPC):
        dma("sync", LD, kx_hi[:], ksd[0, h], reads=(tKS,), writes=(tKxh,))
        dma("sync", LD, kx_lo[:], ksd[1, h], reads=(tKS,), writes=(tKxl,))
        dma("sync", LD, qx_hi[:], qsd[0, h], reads=(tQS,), writes=(tQxh,))
        dma("sync", LD, qx_lo[:], qsd[1, h], reads=(tQS,), writes=(tQxl,))
        for kb in range(NQT):
            dma("sync", LD, vt_hi[:, kb * 128:(kb + 1) * 128],
                vsd[0, h, :, kb * 128:(kb + 1) * 128], reads=(tVS,),
                writes=(tVth,) if kb == NQT - 1 else (), tr=True,
                extra=tVth.ready + tVth.users if kb == 0 else ())
            dma("sync", LD, vt_lo[:, kb * 128:(kb + 1) * 128],
                vsd[1, h, :, kb * 128:(kb + 1) * 128], reads=(tVS,),
                writes=(tVtl,) if kb == NQT - 1 else (), tr=True,
                extra=tVtl.ready + tVtl.users if kb == 0 else ())
        prologue = [True]
        for pr in range(NQT // 2):
            def SC(sl, pr_):
                if sl == 0:
                    return sc_sb[0], tSc[0]
                return (sc_sb[1], tSc[1]) if pr_ % 2 == 0 else (sc_sb[2], tSc[2])
            def emit_scores_part(sl, qt, pr_):
                L = (qt + 1) * 128
                nkc = (L + 511) // 512
                scb, tScb = SC(sl, pr_)
                p.group("vector", [lambda e, o=cm[sl][:]: e.memset(o, 1.0e30)],
                        writes=(tCm[sl],), inc=VE)
                for kc in range(nkc):
                    w = min(512, L - kc * 512)
                    bk_i = kc % 4
                    mm3(bk_i, w, [(qx_hi[:, qt * 128:(qt + 1) * 128],
                                   qx_lo[:, qt * 128:(qt + 1) * 128],
                                   kx_hi[:, kc * 512:kc * 512 + w],
                                   kx_lo[:, kc * 512:kc * 512 + w])],
                        (tQxh, tQxl, tKxh, tKxl))
                    last = (kc == nkc - 1)
                    wm = w - 128 if last else w
                    if wm > 0:
                        p.group("scalar", [lambda e, o=scb[:, kc * 512:
                                kc * 512 + wm], i=pbank[bk_i][:, 0:wm]:
                                e.activation(o, i, AF.Identity, bias=0.0,
                                             scale=1.0)],
                                reads=(tPB[bk_i],), writes=(tScb,),
                                inc=SE, accum=True)
                    if last:
                        p.group("vector", [lambda e, o=scb[:, L - 128:L],
                                i=pbank[bk_i][:, w - 128:w], m=maskt[:]:
                                e.tensor_tensor(o, i, m, OP.add)],
                                reads=(tPB[bk_i], tMk), writes=(tScb,),
                                inc=VE, accum=True)
                    p.group("vector", [lambda e, o=cm[sl][:, kc:kc + 1],
                            i=scb[:, kc * 512:kc * 512 + w]:
                            e.reduce_max(o, i, axis=AX.X, negate=True)],
                            reads=(tScb,), writes=(tCm[sl],), inc=VE,
                            accum=True)

            def emit_softmax_part(sl, qt, pr_):
                L = (qt + 1) * 128
                nkc = (L + 511) // 512
                scb, tScb = SC(sl, pr_)
                if _os0.environ.get("NOCHUNK"):
                    p.group("vector", [lambda e, o=nm[sl][:], i=scb[:, 0:L]:
                            e.reduce_max(o, i, axis=AX.X, negate=True)],
                            reads=(tScb,), writes=(tNm[sl],), inc=VE)
                else:
                    p.group("vector", [lambda e, o=nm[sl][:], i=cm[sl][:]:
                            e.tensor_reduce(o, i, axis=AX.X, op=OP.min)],
                            reads=(tCm[sl],), writes=(tNm[sl],), inc=VE)
                p.group("scalar", [lambda e, o=nms[sl][:], i=nm[sl][:]:
                        e.activation(o, i, AF.Identity, bias=0.0, scale=INVSQ)],
                        reads=(tNm[sl],), writes=(tNms[sl],), inc=SE)
                p.group("scalar", [lambda e, o=p_sb[sl][:, 0:L],
                        i=scb[:, 0:L], b=nms[sl][:, 0:1], a=ssum[sl][:]:
                        e.activation(o, i, AF.Exp, bias=b, scale=INVSQ,
                                     accum_out=a)],
                        reads=(tScb, tNms[sl]), writes=(tP[sl], tSs[sl]), inc=SE)
                p.group("vector", [lambda e, o=rcp[sl][:], i=ssum[sl][:]:
                        e.reciprocal(o, i)],
                        reads=(tSs[sl],), writes=(tRcp[sl],), inc=VE)
                p.group("vector", [lambda e, o=scb[:, 0:L],
                        i=p_sb[sl][:, 0:L], r=rcp[sl][:, 0:1]:
                        e.tensor_scalar_mul(o, i, r)],
                        reads=(tP[sl], tRcp[sl]), writes=(tScb,), inc=VE)

            def emit_pt_part(sl, qt, pr_):
                nkb = qt + 1
                scb, tScb = SC(sl, pr_)
                for kbg in range((nkb + 3) // 4):
                    n4 = min(4, nkb - kbg * 4)
                    bk_i = 4 + (kbg % 2)
                    transposes(bk_i, [scb[:, (kbg * 4 + k) * 128:
                                               (kbg * 4 + k + 1) * 128]
                                      for k in range(n4)], (tScb,))
                    split_psum(bk_i, n4 * 128,
                               pt_hi[sl][:, kbg * 512:kbg * 512 + n4 * 128],
                               pt_lo[sl][:, kbg * 512:kbg * 512 + n4 * 128],
                               tPth[sl], tPtl[sl], accum=True)

            def emit_pv_part(sl, qt):
                nkb = qt + 1
                bk_i = 6 + sl
                pairs = [(vt_hi[:, kb * 128:(kb + 1) * 128],
                          vt_lo[:, kb * 128:(kb + 1) * 128],
                          pt_hi[sl][:, kb * 128:(kb + 1) * 128],
                          pt_lo[sl][:, kb * 128:(kb + 1) * 128])
                         for kb in range(nkb)]
                mm3(bk_i, 128, pairs, (tVth, tVtl, tPth[sl], tPtl[sl]))
                split_psum(bk_i, 128, aoh[sl][:], aol[sl][:], tAoh[sl], tAol[sl])
                dma("gpsimd", ST, aod[0, h, :, qt * 128:(qt + 1) * 128],
                    aoh[sl][:], reads=(tAoh[sl],))
                dma("gpsimd", ST, aod[1, h, :, qt * 128:(qt + 1) * 128],
                    aol[sl][:], reads=(tAol[sl],))
                tAOD.ready.append((ST, ST.v))

            if prologue[0]:
                emit_scores_part(0, pr, pr)
                emit_scores_part(1, NQT - 1 - pr, pr)
                prologue[0] = False
            emit_softmax_part(0, pr, pr)
            emit_pt_part(0, pr, pr)
            emit_softmax_part(1, NQT - 1 - pr, pr)
            emit_pv_part(0, pr)
            if pr + 1 < NQT // 2:
                emit_scores_part(0, pr + 1, pr + 1)
                emit_scores_part(1, NQT - 2 - pr, pr + 1)
            emit_pt_part(1, NQT - 1 - pr, pr)
            emit_pv_part(1, NQT - 1 - pr)
            for _ in range(prep_per_pair):
                emit_prep_unit()
    while prep_pos[0] < len(prep_units):
        emit_prep_unit()

    p.barrier()
    ctxB.close()

    ctxB2 = ExitStack()
    oT_hi, tOTh = mksb(ctxB2, "oT_hi", [128, HPC * D], BF16)
    oT_lo, tOTl = mksb(ctxB2, "oT_lo", [128, HPC * D], BF16)
    won = []; tWon = []
    for i in range(2):
        a_, ta_ = mksb(ctxB2, f"won{i}", [128, D]); won.append(a_); tWon.append(ta_)
    aot_hi = []; aot_lo = []; tAth = []; tAtl = []
    for i in range(2):
        a_, ta_ = mksb(ctxB2, f"aoth{i}", [128, HPC * 128], BF16)
        aot_hi.append(a_); tAth.append(ta_)
        b_, tb_ = mksb(ctxB2, f"aotl{i}", [128, HPC * 128], BF16)
        aot_lo.append(b_); tAtl.append(tb_)
    pos_ = []; tPos = []
    for i in range(2):
        a_, ta_ = mksb(ctxB2, f"pos{i}", [128, D]); pos_.append(a_); tPos.append(ta_)

    for hc in range(HPC):
        buf = hc % 2
        dma("sync", LD, won[buf][:], wo[hc], writes=(tWon[buf],))
        split_sbuf(won[buf][:], tWon[buf], oT_hi[:, hc * D:(hc + 1) * D],
                   oT_lo[:, hc * D:(hc + 1) * D], tOTh, tOTl, accum=True)

    NOG = D // 512
    tCCIN = _T()
    for tt in range(S // 128):
        buf = tt % 2
        for hc in range(HPC):
            dma("sync", LD, aot_hi[buf][:, hc * 128:(hc + 1) * 128],
                aod[0, hc, :, tt * 128:(tt + 1) * 128], reads=(tAOD,),
                writes=(tAth[buf],) if hc == HPC - 1 else (),
                extra=tAth[buf].ready + tAth[buf].users if hc == 0 else ())
            dma("sync", LD, aot_lo[buf][:, hc * 128:(hc + 1) * 128],
                aod[1, hc, :, tt * 128:(tt + 1) * 128], reads=(tAOD,),
                writes=(tAtl[buf],) if hc == HPC - 1 else (),
                extra=tAtl[buf].ready + tAtl[buf].users if hc == 0 else ())
        rb = (tt * 128) // OT
        off = (tt * 128) % OT
        sb_ = tt % 2
        for og in range(NOG):
            bk_i = og % 4
            pairs = []
            for hc in range(HPC):
                pairs.append((aot_hi[buf][:, hc * 128:(hc + 1) * 128],
                              aot_lo[buf][:, hc * 128:(hc + 1) * 128],
                              oT_hi[:, hc * D + og * 512:hc * D + (og + 1) * 512],
                              oT_lo[:, hc * D + og * 512:hc * D + (og + 1) * 512]))
            mm3(bk_i, 512, pairs, (tOTh, tOTl, tAth[buf], tAtl[buf]))
            p.group("scalar", [lambda e, o=pos_[sb_][:, og * 512:(og + 1) * 512],
                    i=pbank[bk_i][:, 0:512]:
                    e.activation(o, i, AF.Identity, bias=0.0, scale=1.0)],
                    reads=(tPB[bk_i],), writes=(tPos[sb_],), inc=SE, accum=True)
        dma("gpsimd", ST, cc_in.ap()[rb, off:off + 128, :], pos_[sb_][:],
            reads=(tPos[sb_],))
        tCCIN.ready.append((ST, ST.v))

    if _os0.environ.get("DBG_CCIN"):
        dbgci = nc.dram_tensor("dbgci", [G, OT, D], F32, kind="ExternalOutput").ap()
        p.group("gpsimd", [lambda e: e.dma_start(out=dbgci, in_=cc_in.ap())],
                inc=OU, amt=16, reads=(tCCIN,))
        tCCIN.ready.append((OU, OU.v))
    fl_out = nc.dram_tensor("fl_out", [128], F32)
    p.group("gpsimd", [lambda e: e.collective_compute(
        "ReduceScatter", OP.add,
        replica_groups=[[0, 1, 2, 3], [4, 5, 6, 7]],
        ins=[cc_in.ap()], outs=[cc_out.ap()])],
        reads=(tCCIN,), inc=CC)
    p.group("gpsimd", [lambda e: e.collective_compute(
        "AllReduce", OP.add,
        replica_groups=[[0, 1, 2, 3, 4, 5, 6, 7]],
        ins=[cc_out.ap().rearrange("a b -> (a b)")[0:128]],
        outs=[fl_out.ap()])],
        extra_waits=[(CC, CC.v)], inc=CC)
    tCCO = _T()
    tCCO.ready = [(CC, CC.v)]

    p.barrier(exclude=("CC",))
    ctxB2.close()

    ctxC = ExitStack()
    xq_hi, tXqh = mksb(ctxC, "xq_hi", [128, IC * TQ], BF16)
    xq_lo, tXql = mksb(ctxC, "xq_lo", [128, IC * TQ], BF16)
    xq_tm = []; tXqtm = []
    for i in range(1):
        a_, ta_ = mksb(ctxC, f"xqtm{i}", [128, D]); xq_tm.append(a_); tXqtm.append(ta_)
    xo_sub = []; tXos = []
    for i in range(2):
        a_, ta_ = mksb(ctxC, f"xos{i}", [128, D]); xo_sub.append(a_); tXos.append(ta_)
    h_hi, tHh = mksb(ctxC, "h_hi", [128, FH * TQ], BF16)
    h_lo, tHl = mksb(ctxC, "h_lo", [128, FH * TQ], BF16)
    hf_sb, tHf = mksb(ctxC, "hf", [128, TQ])
    w1b_hi = []; w1b_lo = []; tW1h = []; tW1l = []
    for i in range(2):
        a_, ta_ = mksb(ctxC, f"w1bh{i}", [128, IC * 128], BF16); w1b_hi.append(a_); tW1h.append(ta_)
        b_, tb_ = mksb(ctxC, f"w1bl{i}", [128, IC * 128], BF16); w1b_lo.append(b_); tW1l.append(tb_)
    w2b_hi = []; w2b_lo = []; tW2h = []; tW2l = []
    for i in range(2):
        a_, ta_ = mksb(ctxC, f"w2bh{i}", [128, D], BF16); w2b_hi.append(a_); tW2h.append(ta_)
        b_, tb_ = mksb(ctxC, f"w2bl{i}", [128, D], BF16); w2b_lo.append(b_); tW2l.append(tb_)
    g_tm = []; tGtm = []
    s_t = []; tSt = []
    outst = []; tOut = []
    for i in range(2):
        a_, ta_ = mksb(ctxC, f"gtm{i}", [128, D]); g_tm.append(a_); tGtm.append(ta_)
        b_, tb_ = mksb(ctxC, f"st{i}", [128, D]); s_t.append(b_); tSt.append(tb_)
        d_, td_ = mksb(ctxC, f"outst{i}", [128, D]); outst.append(d_); tOut.append(td_)
    lnC = ln_pipeline(ctxC, "lnC", scr_pair=(xq_tm[0], tXqtm[0]))

    ccw = [(CC, CC.v)]
    NQV = OT // TQ
    NSQ = TQ // 128
    for qv in range(NQV):
        xqr_hi = xq_hi[:].rearrange("q (i c) -> q i c", c=TQ)
        xqr_lo = xq_lo[:].rearrange("q (i c) -> q i c", c=TQ)
        for a2 in range(NSQ):
            buf = a2 % 2
            g = qv * TQ + a2 * 128
            dma("sync", LD, xo_sub[buf][:], xown[g:g + 128, :], writes=(tXos[buf],),
                extra=ccw)
            lnC(xo_sub[buf][:], tXos[buf], xq_tm[0][:], tXqtm[0])
            for icg in range(IC // 4):
                bk_i = 6 + (icg % 2)
                transposes(bk_i, [xq_tm[0][:, (icg * 4 + k) * 128:
                                             (icg * 4 + k + 1) * 128]
                                  for k in range(4)], (tXqtm[0],))
                hi_ap = xqr_hi[:, icg * 4:(icg + 1) * 4, a2 * 128:(a2 + 1) * 128]
                lo_ap = xqr_lo[:, icg * 4:(icg + 1) * 4, a2 * 128:(a2 + 1) * 128]
                split_psum(bk_i, 512, hi_ap, lo_ap, tXqh, tXql, accum=True,
                           three_d=True)
        for hc in range(FH):
            buf = hc % 2
            dma("sync", LD, w1b_hi[buf][:], w1td[0, hc], reads=(tW1T,),
                writes=(tW1h[buf],), extra=ccw)
            dma("sync", LD, w1b_lo[buf][:], w1td[1, hc], reads=(tW1T,),
                writes=(tW1l[buf],), extra=ccw)
            bk_i = (0, 1, 4, 5)[hc % 4]
            pairs = []
            for ic in range(IC):
                pairs.append((w1b_hi[buf][:, ic * 128:(ic + 1) * 128],
                              w1b_lo[buf][:, ic * 128:(ic + 1) * 128],
                              xq_hi[:, ic * TQ:(ic + 1) * TQ],
                              xq_lo[:, ic * TQ:(ic + 1) * TQ]))
            mm3(bk_i, TQ, pairs, (tW1h[buf], tW1l[buf], tXqh, tXql))
            p.group("scalar", [lambda e, o=hf_sb[:], i=pbank[bk_i][:, 0:TQ],
                    b=b1_sb[:, hc:hc + 1]: e.activation(o, i, AF.Gelu, bias=b,
                                                        scale=1.0)],
                    reads=(tPB[bk_i], tB1), writes=(tHf,), inc=SE)
            split_sbuf(hf_sb[:], tHf, h_hi[:, hc * TQ:(hc + 1) * TQ],
                       h_lo[:, hc * TQ:(hc + 1) * TQ], tHh, tHl, accum=True)
        for hc in range(FH):
            buf = hc % 2
            dma("sync", LD, w2b_hi[buf][:], w2td[0, hc], reads=(tW2T,),
                writes=(tW2h[buf],), extra=ccw)
            dma("sync", LD, w2b_lo[buf][:], w2td[1, hc], reads=(tW2T,),
                writes=(tW2l[buf],), extra=ccw)
            fns = []
            rd = (tW2h[buf], tW2l[buf], tHh, tHl)
            first = (hc == 0)
            last = (hc == FH - 1)
            for t2 in range(NSQ):
                for og in range(D // 512):
                    bk_i = og * NSQ + t2
                    lh = h_hi[:, hc * TQ + t2 * 128:hc * TQ + (t2 + 1) * 128]
                    ll = h_lo[:, hc * TQ + t2 * 128:hc * TQ + (t2 + 1) * 128]
                    rh = w2b_hi[buf][:, og * 512:(og + 1) * 512]
                    rl = w2b_lo[buf][:, og * 512:(og + 1) * 512]
                    for pi, (a_, b_) in enumerate(((lh, rh), (lh, rl),
                                                   (ll, rh))):
                        fns.append(lambda e, o=pbank[bk_i][:, 0:512], x=a_,
                                   y=b_, s=(first and pi == 0),
                                   t=(last and pi == 2):
                                   e.matmul(o, x, y, start=s, stop=t))
            if first:
                ws = []
                for bi in range(NSQ * (D // 512)):
                    ws += tPB[bi].ready + tPB[bi].users
                p.group("tensor", fns, reads=rd, extra_waits=ws, inc=MM)
            elif last:
                p.group("tensor", fns, reads=rd, inc=MM)
                for bi in range(NSQ * (D // 512)):
                    tPB[bi].ready = [(MM, MM.v)]
                    tPB[bi].users = []
            else:
                p.group("tensor", fns, reads=rd, inc=MM)
        for t2 in range(NSQ):
            for og in range(D // 512):
                bk_i = og * NSQ + t2
                p.group("vector", [lambda e, o=s_t[t2 % 2][:, og * 512:(og + 1) * 512],
                        i=pbank[bk_i][:, 0:512], b=b2b_sb[:, og * 512:(og + 1) * 512]:
                        e.tensor_tensor(o, i, b, OP.add)],
                        reads=(tPB[bk_i], tB2b), writes=(tSt[t2 % 2],), inc=VE,
                        accum=True)
            p.group("scalar", [lambda e, o=g_tm[t2][:], i=s_t[t2 % 2][:]:
                    e.activation(o, i, AF.Silu, bias=0.0, scale=1.0)],
                    reads=(tSt[t2 % 2],), writes=(tGtm[t2],), inc=SE)
        for t2 in range(NSQ):
            buf = t2 % 2
            g0 = qv * TQ + t2 * 128
            dma("sync", LD, s_t[buf][:], cc_out.ap()[g0:g0 + 128, :],
                reads=(tCCO,), writes=(tSt[buf],), extra=ccw)
            p.group("vector", [lambda e, o=outst[buf][:], i=s_t[buf][:],
                    b=bo_sb[:]: e.tensor_tensor(o, i, b, OP.add)],
                    reads=(tSt[buf], tBo), writes=(tOut[buf],), inc=VE)
            p.group("vector", [lambda e, o=s_t[buf][:], i=outst[buf][:],
                    g_=g_tm[t2][:]: e.tensor_tensor(o, i, g_, OP.mult)],
                    reads=(tOut[buf], tGtm[t2]), writes=(tSt[buf],), inc=VE)
            p.group("vector", [lambda e, o=outst[buf][:], i=s_t[buf][:],
                    x=xo_sub[t2 % 2][:]: e.tensor_tensor(o, i, x, OP.add)],
                    reads=(tSt[buf], tXos[t2 % 2]), writes=(tOut[buf],), inc=VE)
            dma("gpsimd", OU, outd[g0:g0 + 128, :], outst[buf][:],
                reads=(tOut[buf],))

    if _os0.environ.get("DBG_SG"):
        dbgs = nc.dram_tensor("dbgs", [OT, D], F32, kind="ExternalOutput").ap()
        allw = [(s_, s_.v) for s_ in p.sems if s_.v > 0]
        p.group("gpsimd", [lambda e: e.dma_start(out=dbgs, in_=cc_out.ap())],
                inc=OU, amt=16, extra_waits=allw)
    p.final_waits = [(OU.name, OU.v), (ST.name, ST.v)]
    ctxC.close()

    hsem = {}
    for s in p.sems:
        hsem[s.name] = gctx.enter_context(nc.semaphore(s.name))
    _emit(nc, p, hsem)
    gctx.close()
    return nc


def _in_maps(cfg, x, q_w, k_w, v_w, o_w, q_b, k_b, v_b, o_b, w1, b1, w2, b2):
    S, D, F, HPC = cfg["S"], cfg["D"], cfg["F"], cfg["HPC"]
    OT = S // 4
    IC = D // 128
    FH = F // 128
    ident = np.eye(128, dtype=np.float32)
    ii, jj = np.indices((128, 128))
    mask = np.where(jj > ii, np.float32(NEG), np.float32(0.0))
    maps = []
    c32 = np.ascontiguousarray

    def tile_oc(w):
        O, I = w.shape
        return c32(w.reshape(O // 128, 128, I // 128, 128).transpose(0, 3, 2, 1))

    w1t_host = tile_oc(w1)
    D_ = w2.shape[0]
    F_ = w2.shape[1]
    w2t_host = c32(w2.T).reshape(F_ // 128, 128, D_)
    for c in range(8):
        b = c // 4
        j = c % 4
        H0 = j * HPC * 128
        HD = HPC * 128
        maps.append({
            "xb": c32(x[b]),
            "xown": c32(x[b, j * OT:(j + 1) * OT]),
            "wq": tile_oc(q_w[H0:H0 + HD]),
            "wk": tile_oc(k_w[H0:H0 + HD]),
            "wv": tile_oc(v_w[H0:H0 + HD]),
            "wo": c32(o_w[:, H0:H0 + HD].T).reshape(HPC, 128, D_),
            "w1": w1t_host,
            "w2": w2t_host,
            "bqs": c32(q_b[H0:H0 + HD].reshape(HPC, 128).T),
            "bks": c32(k_b[H0:H0 + HD].reshape(HPC, 128).T),
            "bvs": c32(v_b[H0:H0 + HD].reshape(HPC, 128).T),
            "bos": c32(np.broadcast_to(o_b, (128, D_))),
            "b1s": c32(b1.reshape(FH, 128).T),
            "b2s": c32(b2.reshape(IC, 128).T),
            "b2bc": c32(np.broadcast_to(b2, (128, D_))),
            "identd": ident,
            "maskd": mask,
        })
    return maps


def run(cfg, x, q_w, k_w, v_w, o_w, q_b, k_b, v_b, o_b, w1, b1, w2, b2):
    nc = build(cfg)
    maps = _in_maps(cfg, x, q_w, k_w, v_w, o_w, q_b, k_b, v_b, o_b, w1, b1, w2, b2)
    res = run_bass_kernel_spmd(nc, maps, core_ids=list(range(8)))
    S, D = cfg["S"], cfg["D"]
    OT = S // 4
    out = np.empty((2, S, D), np.float32)
    for c in range(8):
        out[c // 4, (c % 4) * OT:(c % 4 + 1) * OT] = res.results[c]["out"]
    return out


def kernel(L0_attn_hidden_states, L0_attn_q_w, L0_attn_k_w, L0_attn_v_w,
           L0_attn_o_w, L0_attn_q_b, L0_attn_k_b, L0_attn_v_b, L0_attn_o_b,
           L0_ffn_proj1, L0_ffn_proj1_bias, L0_ffn_proj2, L0_ffn_proj2_bias,
           L0_ffn_gate_w):
    cfg = dict(S=4096, D=2048, F=8192, HPC=4, TB=512, TQ=256)
    return run(cfg, np.asarray(L0_attn_hidden_states),
               np.asarray(L0_attn_q_w), np.asarray(L0_attn_k_w),
               np.asarray(L0_attn_v_w), np.asarray(L0_attn_o_w),
               np.asarray(L0_attn_q_b), np.asarray(L0_attn_k_b),
               np.asarray(L0_attn_v_b), np.asarray(L0_attn_o_b),
               np.asarray(L0_ffn_proj1), np.asarray(L0_ffn_proj1_bias),
               np.asarray(L0_ffn_proj2), np.asarray(L0_ffn_proj2_bias))


# version 12 (final)
# speedup vs baseline: 1.0345x; 1.0345x over previous
import math
import numpy as np
from contextlib import ExitStack

import concourse.bass as bass
import concourse.mybir as mybir
from concourse.bass_utils import run_bass_kernel_spmd

F32 = mybir.dt.float32
BF16 = mybir.dt.bfloat16
AF = mybir.ActivationFunctionType
AX = mybir.AxisListType
OP = mybir.AluOpType

NEG = -1.0e30


class _Sem:
    __slots__ = ("name", "v")

    def __init__(self, name):
        self.name = name
        self.v = 0


class _T:
    __slots__ = ("ready", "users")

    def __init__(self):
        self.ready = []
        self.users = []


class _Prog:
    def __init__(self):
        self.ops = []
        self.sems = []
        self.pw = []
        self.final_waits = []

    def sem(self, name):
        s = _Sem(name)
        self.sems.append(s)
        return s

    def barrier(self, exclude=()):
        self.pw = [(s, s.v) for s in self.sems if s.v > 0 and s.name not in exclude]

    def group(self, eng, fns, reads=(), writes=(), inc=None, amt=1,
              extra_waits=(), accum=False):
        ws = list(extra_waits) + list(self.pw)
        for t in reads:
            ws += t.ready
        for t in writes:
            ws += t.ready + t.users
        wd = {}
        for (s, v) in ws:
            assert v <= s.v, f"wait on future {s.name}: {v} > {s.v}"
            if wd.get(s, 0) < v:
                wd[s] = v
        wt = tuple((s.name, v) for s, v in wd.items())
        assert inc is not None
        inc.v += amt
        iv = (inc.name, amt)
        for i, fn in enumerate(fns):
            self.ops.append((eng, fn,
                             wt if i == 0 else (),
                             iv if i == len(fns) - 1 else None))
        ev = (inc, inc.v)
        for t in reads:
            t.users.append(ev)
        for t in writes:
            if accum:
                t.ready.append(ev)
            else:
                t.ready = [ev]
                t.users = []
        return inc.v


def _emit(nc, prog, hsem):
    engines = ("sync", "scalar", "vector", "tensor", "gpsimd")
    per = {e: [] for e in engines}
    for o in prog.ops:
        per[o[0]].append(o)
    with nc.Block() as block:
        for name in engines:
            lst = per[name]
            fin = prog.final_waits if name == "gpsimd" else []
            if not lst and not fin:
                continue

            def make(lst=lst, fin=fin):
                def run(e):
                    lw = {}
                    for (_, fn, ws, iv) in lst:
                        for (sn, v) in ws:
                            if lw.get(sn, 0) < v:
                                e.wait_ge(hsem[sn], v)
                                lw[sn] = v
                        ins = fn(e)
                        if iv is not None:
                            ins.then_inc(hsem[iv[0]], iv[1])
                    for (sn, v) in fin:
                        e.wait_ge(hsem[sn], v)
                return run

            getattr(block, name)(make())


def build(cfg):
    import os as _os0
    S = cfg["S"]; D = cfg["D"]; F = cfg["F"]; HPC = cfg["HPC"]
    G = 4
    OT = S // G
    TB = cfg["TB"]; TQ = cfg["TQ"]
    IC = D // 128
    FH = F // 128
    NTB = S // TB
    NSUB = TB // 128
    NQT = S // 128
    HD = HPC * 128
    F2 = F // 2
    INVSQ = 1.0 / math.sqrt(128.0)
    assert NQT % 2 == 0 and OT % TQ == 0 and TB % 128 == 0 and S % TB == 0
    assert TQ % 128 == 0 and TB <= 512 and TQ <= 512 and IC % 4 == 0

    nc = bass.Bass("TRN2", target_bir_lowering=False)

    xb = nc.dram_tensor("xb", [S, D], F32, kind="ExternalInput").ap()
    xown = nc.dram_tensor("xown", [OT, D], F32, kind="ExternalInput").ap()
    wq = nc.dram_tensor("wq", [HPC, 128, IC * 128], F32, kind="ExternalInput").ap()
    wk = nc.dram_tensor("wk", [HPC, 128, IC * 128], F32, kind="ExternalInput").ap()
    wv = nc.dram_tensor("wv", [HPC, 128, IC * 128], F32, kind="ExternalInput").ap()
    wo = nc.dram_tensor("wo", [HPC, 128, D], F32, kind="ExternalInput").ap()
    w1 = nc.dram_tensor("w1", [FH, 128, IC * 128], F32, kind="ExternalInput").ap()
    w2 = nc.dram_tensor("w2", [FH, 128, D], F32, kind="ExternalInput").ap()
    bqs = nc.dram_tensor("bqs", [128, HPC], F32, kind="ExternalInput").ap()
    bks = nc.dram_tensor("bks", [128, HPC], F32, kind="ExternalInput").ap()
    bvs = nc.dram_tensor("bvs", [128, HPC], F32, kind="ExternalInput").ap()
    bos = nc.dram_tensor("bos", [128, D], F32, kind="ExternalInput").ap()
    b2bc = nc.dram_tensor("b2bc", [128, D], F32, kind="ExternalInput").ap()
    b1s = nc.dram_tensor("b1s", [128, FH], F32, kind="ExternalInput").ap()
    b2s = nc.dram_tensor("b2s", [128, IC], F32, kind="ExternalInput").ap()
    identd = nc.dram_tensor("identd", [128, 128], F32, kind="ExternalInput").ap()
    maskd = nc.dram_tensor("maskd", [128, 128], F32, kind="ExternalInput").ap()
    outd = nc.dram_tensor("out", [OT, D], F32, kind="ExternalOutput").ap()

    qsd = nc.dram_tensor("qsd", [2, HPC, 128, S], BF16).ap()
    ksd = nc.dram_tensor("ksd", [2, HPC, 128, S], BF16).ap()
    vsd = nc.dram_tensor("vsd", [2, HPC, 128, S], BF16).ap()
    aod = nc.dram_tensor("aod", [2, HPC, 128, S], BF16).ap()
    w1td = nc.dram_tensor("w1td", [2, FH, 128, IC * 128], BF16).ap()
    w2td = nc.dram_tensor("w2td", [2, FH, 128, D], BF16).ap()
    cc_in = nc.dram_tensor("cc_in", [G, OT, D], F32)
    cc_out = nc.dram_tensor("cc_out", [OT, D], F32)

    p = _Prog()
    LD = p.sem("LD"); ST = p.sem("ST"); OU = p.sem("OU")
    MM = p.sem("MM"); VE = p.sem("VE"); SE = p.sem("SE"); CC = p.sem("CC")
    CN = p.sem("CN"); HK = p.sem("HK"); HV = p.sem("HV")
    WN = [p.sem("WN0"), p.sem("WN1")]; XS = [p.sem("XS0"), p.sem("XS1")]
    WP = [p.sem("WP0"), p.sem("WP1")]; AT = [p.sem("AT0"), p.sem("AT1")]
    W2S = [p.sem("W20"), p.sem("W21")]; SCS = [p.sem("SC0"), p.sem("SC1")]
    QS = [p.sem("QS0"), p.sem("QS1")]; PS = [p.sem("PS0"), p.sem("PS1")]
    AO = [p.sem("AO0"), p.sem("AO1")]; CI = [p.sem("CI0"), p.sem("CI1")]
    OD = [p.sem("OD0"), p.sem("OD1")]

    gctx = ExitStack()

    def mksb(ctx, name, shape, dt=F32):
        return ctx.enter_context(nc.sbuf_tensor(name, shape, dt)), _T()

    ident, tI = mksb(gctx, "ident", [128, 128])
    maskt, tMk = mksb(gctx, "maskt", [128, 128])
    epst, tEp = mksb(gctx, "epst", [128, 1])
    bq_sb, tBq = mksb(gctx, "bq_sb", [128, HPC])
    bk_sb, tBk = mksb(gctx, "bk_sb", [128, HPC])
    bv_sb, tBv = mksb(gctx, "bv_sb", [128, HPC])
    bo_sb, tBo = mksb(gctx, "bo_sb", [128, D])
    b2b_sb, tB2b = mksb(gctx, "b2b_sb", [128, D])
    b1_sb, tB1 = mksb(gctx, "b1_sb", [128, FH])
    b2_sb, tB2 = mksb(gctx, "b2_sb", [128, IC])

    pbank = [gctx.enter_context(nc.psum_tensor(f"pb{i}", [128, 512], F32))
             for i in range(8)]
    tPB = [_T() for _ in range(8)]

    def dma(eng, sem, out_ap, in_ap, reads=(), writes=(), extra=(), tr=False):
        if tr:
            fn = lambda e, o=out_ap, i=in_ap: e.dma_start(out=o, in_=i, transpose=True)
        else:
            fn = lambda e, o=out_ap, i=in_ap: e.dma_start(out=o, in_=i)
        return p.group(eng, [fn], reads=reads, writes=writes, inc=sem, amt=16,
                       extra_waits=extra)

    for (t_, sbuf_, dram_) in ((tI, ident, identd), (tMk, maskt, maskd),
                               (tBq, bq_sb, bqs), (tBk, bk_sb, bks),
                               (tBv, bv_sb, bvs), (tBo, bo_sb, bos),
                               (tB2b, b2b_sb, b2bc),
                               (tB1, b1_sb, b1s), (tB2, b2_sb, b2s)):
        dma("sync", CN, sbuf_[:], dram_, writes=(t_,))
    for (t_, sbuf_, dram_) in ((tI, ident, identd), (tMk, maskt, maskd),
                               (tBq, bq_sb, bqs), (tBk, bk_sb, bks),
                               (tBv, bv_sb, bvs), (tBo, bo_sb, bos),
                               (tB2b, b2b_sb, b2bc),
                               (tB1, b1_sb, b1s), (tB2, b2_sb, b2s)):
        t_.ready = [(CN, CN.v)]
    p.group("vector", [lambda e: e.memset(epst[:], 1e-5)], writes=(tEp,), inc=VE)

    def transposes(bank_i, srcs, reads):
        fns = []
        for k, s_ in enumerate(srcs):
            fns.append(lambda e, b=pbank[bank_i][:, k * 128:(k + 1) * 128],
                       s=s_, i=ident[:]: e.transpose(b, s, i))
        p.group("tensor", fns, reads=tuple(reads) + (tI,), writes=(tPB[bank_i],),
                inc=MM)
        return tPB[bank_i]

    def split_psum(bank_i, cols, hi_ap, lo_ap, hi_t, lo_t, accum=False,
                   three_d=False):
        if three_d:
            src = pbank[bank_i][:].rearrange("p (x c) -> p x c", c=128)[:, 0:cols // 128, :]
        else:
            src = pbank[bank_i][:, 0:cols]
        tb = tPB[bank_i]
        p.group("scalar", [lambda e, o=hi_ap, i=src: e.activation(
            o, i, AF.Identity, bias=0.0, scale=1.0)], reads=(tb,), writes=(hi_t,),
            inc=SE, accum=accum)
        p.group("vector", [lambda e, o=lo_ap, i=src, h=hi_ap: e.tensor_tensor(
            o, i, h, OP.subtract)], reads=(tb, hi_t), writes=(lo_t,), inc=VE,
            accum=accum)

    def split_sbuf(src_ap, src_t, hi_ap, lo_ap, hi_t, lo_t, accum=False):
        p.group("scalar", [lambda e, o=hi_ap, i=src_ap: e.activation(
            o, i, AF.Identity, bias=0.0, scale=1.0)], reads=(src_t,), writes=(hi_t,),
            inc=SE, accum=accum)
        p.group("vector", [lambda e, o=lo_ap, i=src_ap, h=hi_ap: e.tensor_tensor(
            o, i, h, OP.subtract)], reads=(src_t, hi_t), writes=(lo_t,), inc=VE,
            accum=accum)

    def mm3(bank_i, cols, pairs, readsets, extra=()):
        fns = []
        n = 0
        total = len(pairs) * 3
        for (lh, ll, rh, rl) in pairs:
            for (a_, b_) in ((lh, rh), (lh, rl), (ll, rh)):
                n += 1
                fns.append(lambda e, o=pbank[bank_i][:, 0:cols], x=a_, y=b_,
                           s=(n == 1), t=(n == total): e.matmul(o, x, y,
                                                               start=s, stop=t))
        p.group("tensor", fns, reads=tuple(readsets), writes=(tPB[bank_i],),
                inc=MM, extra_waits=extra)
        return tPB[bank_i]

    def ln_pipeline(ctx, tag, scr_pair=None):
        if scr_pair is None:
            scr, tScr = mksb(ctx, f"{tag}_scr", [128, D])
        else:
            scr, tScr = scr_pair
        s1, tS1 = mksb(ctx, f"{tag}_s1", [128, 1])
        s2, tS2 = mksb(ctx, f"{tag}_s2", [128, 1])
        mean, tMean = mksb(ctx, f"{tag}_mean", [128, 1])
        nmean, tNmean = mksb(ctx, f"{tag}_nmean", [128, 1])
        std, tStd = mksb(ctx, f"{tag}_std", [128, 1])
        rstd, tRstd = mksb(ctx, f"{tag}_rstd", [128, 1])

        def ln(x_ap, x_t, out_ap, out_t):
            p.group("vector", [lambda e: e.reduce_sum(s1[:], x_ap, axis=AX.X)],
                    reads=(x_t,), writes=(tS1,), inc=VE)
            p.group("scalar", [lambda e: e.activation(nmean[:], s1[:], AF.Identity,
                    bias=0.0, scale=-1.0 / D)], reads=(tS1,), writes=(tNmean,),
                    inc=SE)
            p.group("scalar", [lambda e: e.activation(mean[:], s1[:], AF.Identity,
                    bias=0.0, scale=1.0 / D)], reads=(tS1,), writes=(tMean,), inc=SE)
            p.group("scalar", [lambda e: e.activation(scr[:], x_ap, AF.Square,
                    bias=nmean[:, 0:1], scale=1.0, accum_out=s2[:])],
                    reads=(x_t, tNmean), writes=(tScr, tS2), inc=SE)
            p.group("scalar", [lambda e: e.activation(std[:], s2[:], AF.Sqrt,
                    bias=epst[:, 0:1], scale=1.0 / D)], reads=(tS2, tEp),
                    writes=(tStd,), inc=SE)
            p.group("vector", [lambda e: e.reciprocal(rstd[:], std[:])],
                    reads=(tStd,), writes=(tRstd,), inc=VE)
            p.group("vector", [lambda e: e.tensor_scalar(out_ap, x_ap,
                    mean[:, 0:1], rstd[:, 0:1], OP.subtract, OP.mult)],
                    reads=(x_t, tMean, tRstd), writes=(out_t,), inc=VE)
        return ln

    ctxA = ExitStack()
    wT_hi = []; wT_lo = []; tWThi = []; tWTlo = []
    for t in range(3):
        h_, th_ = mksb(ctxA, f"wTh{t}", [128, IC * HPC * 128], BF16)
        l_, tl_ = mksb(ctxA, f"wTl{t}", [128, IC * HPC * 128], BF16)
        wT_hi.append(h_); wT_lo.append(l_); tWThi.append(th_); tWTlo.append(tl_)
    wn_sb = []; tWn = []
    for i in range(2):
        b_, t_ = mksb(ctxA, f"wn{i}", [128, D])
        wn_sb.append(b_); tWn.append(t_)

    wsrc = [wq, wk, wv]
    for t in range(3):
        for oc in range(HPC):
            buf = (t * HPC + oc) % 2
            dma("sync", WN[buf], wn_sb[buf][:, 0:IC * 128], wsrc[t][oc], writes=(tWn[buf],))
            base = oc * IC * 128
            split_sbuf(wn_sb[buf][:, 0:IC * 128], tWn[buf],
                       wT_hi[t][:, base:base + IC * 128],
                       wT_lo[t][:, base:base + IC * 128], tWThi[t], tWTlo[t],
                       accum=True)

    x_sub = []; tXs = []
    xn_tm = []; tXn = []
    for i in range(2):
        a_, ta_ = mksb(ctxA, f"xsub{i}", [128, D]); x_sub.append(a_); tXs.append(ta_)
        b_, tb_ = mksb(ctxA, f"xntm{i}", [128, D]); xn_tm.append(b_); tXn.append(tb_)
    xnf_hi, tXfh = mksb(ctxA, "xnf_hi", [128, IC * TB], BF16)
    xnf_lo, tXfl = mksb(ctxA, "xnf_lo", [128, IC * TB], BF16)
    qf_sb, tQf = mksb(ctxA, "qf", [128, TB])
    stg_hi = []; stg_lo = []; tSgh = []; tSgl = []
    for i in range(2):
        a_, ta_ = mksb(ctxA, f"stgh{i}", [128, TB], BF16); stg_hi.append(a_); tSgh.append(ta_)
        b_, tb_ = mksb(ctxA, f"stgl{i}", [128, TB], BF16); stg_lo.append(b_); tSgl.append(tb_)
    lnA = ln_pipeline(ctxA, "lnA")

    dsts = [qsd, ksd, vsd]
    biases = [(bq_sb, tBq), (bk_sb, tBk), (bv_sb, tBv)]
    tQS = _T(); tKS = _T(); tVS = _T()
    sc_trk = [tQS, tKS, tVS]
    sbuf_stage_ctr = 0
    for tb in range(NTB):
        for a in range(NSUB):
            g = tb * NSUB + a
            buf = g % 2
            dma("sync", XS[buf], x_sub[buf][:], xb[g * 128:(g + 1) * 128, :],
                writes=(tXs[buf],))
            lnA(x_sub[buf][:], tXs[buf], xn_tm[buf][:], tXn[buf])
            xnr_hi = xnf_hi[:].rearrange("q (i c) -> q i c", c=TB)
            xnr_lo = xnf_lo[:].rearrange("q (i c) -> q i c", c=TB)
            for icg in range(IC // 4):
                bk_i = 6 + (icg % 2)
                transposes(bk_i, [xn_tm[buf][:, (icg * 4 + k) * 128:
                                             (icg * 4 + k + 1) * 128]
                                  for k in range(4)], (tXn[buf],))
                hi_ap = xnr_hi[:, icg * 4:(icg + 1) * 4, a * 128:(a + 1) * 128]
                lo_ap = xnr_lo[:, icg * 4:(icg + 1) * 4, a * 128:(a + 1) * 128]
                split_psum(bk_i, 512, hi_ap, lo_ap, tXfh, tXfl, accum=True,
                           three_d=True)
        for t in range(3):
            for oc in range(HPC):
                bk_i = (t * HPC + oc) % 2
                pairs = []
                for ic in range(IC):
                    wb = (oc * IC + ic) * 128
                    pairs.append((wT_hi[t][:, wb:wb + 128],
                                  wT_lo[t][:, wb:wb + 128],
                                  xnf_hi[:, ic * TB:(ic + 1) * TB],
                                  xnf_lo[:, ic * TB:(ic + 1) * TB]))
                mm3(bk_i, TB, pairs, (tWThi[t], tWTlo[t], tXfh, tXfl))
                bt, btt = biases[t]
                p.group("scalar", [lambda e, o=qf_sb[:], i=pbank[bk_i][:, 0:TB],
                        b=bt[:, oc:oc + 1]: e.activation(o, i, AF.Identity, bias=b,
                        scale=1.0)], reads=(tPB[bk_i], btt), writes=(tQf,), inc=SE)
                sbuf_stage_ctr += 1
                sbf = sbuf_stage_ctr % 2
                split_sbuf(qf_sb[:], tQf, stg_hi[sbf][:], stg_lo[sbf][:],
                           tSgh[sbf], tSgl[sbf])
                dma("gpsimd", QS[sbf], dsts[t][0, oc, :, tb * TB:(tb + 1) * TB],
                    stg_hi[sbf][:], reads=(tSgh[sbf],))
                dma("gpsimd", QS[sbf], dsts[t][1, oc, :, tb * TB:(tb + 1) * TB],
                    stg_lo[sbf][:], reads=(tSgl[sbf],))
                sc_trk[t].ready.append((QS[sbf], QS[sbf].v))


    p.barrier()
    ctxA.close()

    ctxB = ExitStack()
    kx_hi, tKxh = mksb(ctxB, "kx_hi", [128, S], BF16)
    kx_lo, tKxl = mksb(ctxB, "kx_lo", [128, S], BF16)
    qx_hi, tQxh = mksb(ctxB, "qx_hi", [128, S], BF16)
    qx_lo, tQxl = mksb(ctxB, "qx_lo", [128, S], BF16)
    vt_hi, tVth = mksb(ctxB, "vt_hi", [128, S], BF16)
    vt_lo, tVtl = mksb(ctxB, "vt_lo", [128, S], BF16)
    LMAX = [NQT // 2 * 128, S]
    sc_sb = []; tSc = []; p_sb = []; tP = []
    pt_hi = []; pt_lo = []; tPth = []; tPtl = []
    nm = []; tNm = []; nms = []; tNms = []
    ssum = []; tSs = []; lns_ = []; tLns = []; bb2 = []; tBb2 = []
    aoh = []; aol = []; tAoh = []; tAol = []
    cm = []; tCm = []; rcp = []; tRcp = []
    scT = []
    for sl in range(2):
        a_, ta_ = mksb(ctxB, f"scs{sl}", [128, LMAX[sl]]); sc_sb.append(a_); tSc.append(ta_)
        scT.append([_T() for _ in range((LMAX[sl] + 511) // 512)])
    for sl in range(2):
        b_, tb_ = mksb(ctxB, f"psb{sl}", [128, LMAX[sl]]); p_sb.append(b_); tP.append(tb_)
        c_, tc_ = mksb(ctxB, f"pth{sl}", [128, LMAX[sl]], BF16); pt_hi.append(c_); tPth.append(tc_)
        d_, td_ = mksb(ctxB, f"ptl{sl}", [128, LMAX[sl]], BF16); pt_lo.append(d_); tPtl.append(td_)
        for si, (lst, tl, shape) in enumerate(((nm, tNm, [128, 1]),
                                               (nms, tNms, [128, 1]),
                                               (ssum, tSs, [128, 1]),
                                               (lns_, tLns, [128, 1]),
                                               (bb2, tBb2, [128, 1]))):
            x_, tx_ = mksb(ctxB, f"sm{si}_{sl}", shape)
            lst.append(x_); tl.append(tx_)
        e_, te_ = mksb(ctxB, f"aoh{sl}", [128, 128], BF16); aoh.append(e_); tAoh.append(te_)
        f_, tf_ = mksb(ctxB, f"aol{sl}", [128, 128], BF16); aol.append(f_); tAol.append(tf_)
        g_, tg_ = mksb(ctxB, f"cm{sl}", [128, 8]); cm.append(g_); tCm.append(tg_)
        h2_, th2_ = mksb(ctxB, f"rcp{sl}", [128, 1]); rcp.append(h2_); tRcp.append(th2_)

    wpn = []; tWpn = []
    for i in range(2):
        a_, ta_ = mksb(ctxB, f"wpn{i}", [128, 2048])
        wpn.append(a_); tWpn.append(ta_)
    wps_hi = []; wps_lo = []; tWpsh = []; tWpsl = []
    for i in range(2):
        a_, ta_ = mksb(ctxB, f"wpsh{i}", [128, 2048], BF16); wps_hi.append(a_); tWpsh.append(ta_)
        b_, tb_ = mksb(ctxB, f"wpsl{i}", [128, 2048], BF16); wps_lo.append(b_); tWpsl.append(tb_)
    tW1T = _T(); tW2T = _T()
    W1CW = min(IC * 128, 2048)
    W2CW = min(D, 2048)
    prep_units = []
    for hc in range(FH):
        for qv in range(IC * 128 // W1CW):
            prep_units.append(("w1", hc, qv))
    for hc in range(FH):
        for qv in range(D // W2CW):
            prep_units.append(("w2", hc, qv))
    prep_pos = [0]

    def emit_prep_unit():
        if prep_pos[0] >= len(prep_units):
            return
        kind, a0, qv = prep_units[prep_pos[0]]
        buf = prep_pos[0] % 2
        prep_pos[0] += 1
        if kind == "w1":
            cw = W1CW
            src_ap = w1[a0, :, qv * cw:(qv + 1) * cw]
            dst_hi = w1td[0, a0, :, qv * cw:(qv + 1) * cw]
            dst_lo = w1td[1, a0, :, qv * cw:(qv + 1) * cw]
            trk = tW1T
        else:
            cw = W2CW
            src_ap = w2[a0, :, qv * cw:(qv + 1) * cw]
            dst_hi = w2td[0, a0, :, qv * cw:(qv + 1) * cw]
            dst_lo = w2td[1, a0, :, qv * cw:(qv + 1) * cw]
            trk = tW2T
        dma("sync", WP[buf], wpn[buf][:, 0:cw], src_ap, writes=(tWpn[buf],))
        split_sbuf(wpn[buf][:, 0:cw], tWpn[buf], wps_hi[buf][:, 0:cw],
                   wps_lo[buf][:, 0:cw], tWpsh[buf], tWpsl[buf])
        dma("gpsimd", PS[buf], dst_hi, wps_hi[buf][:, 0:cw], reads=(tWpsh[buf],))
        dma("gpsimd", PS[buf], dst_lo, wps_lo[buf][:, 0:cw], reads=(tWpsl[buf],))
        trk.ready.append((PS[buf], PS[buf].v))

    n_pairs_total = HPC * (NQT // 2)
    prep_per_pair = (len(prep_units) + n_pairs_total - 1) // n_pairs_total

    tAOD = _T()
    n_pairs_total = HPC * (NQT // 2)
    prep_per_pair = (len(prep_units) + n_pairs_total - 1) // n_pairs_total

    for h in range(HPC):
        dma("sync", HK, kx_hi[:], ksd[0, h], reads=(tKS,), writes=(tKxh,))
        dma("sync", HK, kx_lo[:], ksd[1, h], reads=(tKS,), writes=(tKxl,))
        dma("sync", HK, qx_hi[:], qsd[0, h], reads=(tQS,), writes=(tQxh,))
        dma("sync", HK, qx_lo[:], qsd[1, h], reads=(tQS,), writes=(tQxl,))
        for kb in range(NQT):
            dma("sync", HV, vt_hi[:, kb * 128:(kb + 1) * 128],
                vsd[0, h, :, kb * 128:(kb + 1) * 128], reads=(tVS,),
                writes=(tVth,) if kb == NQT - 1 else (), tr=True,
                extra=tVth.ready + tVth.users if kb == 0 else ())
            dma("sync", HV, vt_lo[:, kb * 128:(kb + 1) * 128],
                vsd[1, h, :, kb * 128:(kb + 1) * 128], reads=(tVS,),
                writes=(tVtl,) if kb == NQT - 1 else (), tr=True,
                extra=tVtl.ready + tVtl.users if kb == 0 else ())
        prologue = [True]
        for pr in range(NQT // 2):
            def SC(sl, pr_):
                return sc_sb[sl], scT[sl]
            def emit_scores_part(sl, qt, pr_):
                L = (qt + 1) * 128
                nkc = (L + 511) // 512
                scb, tScb = SC(sl, pr_)
                p.group("vector", [lambda e, o=cm[sl][:]: e.memset(o, 1.0e30)],
                        writes=(tCm[sl],), inc=VE)
                for kc in range(nkc):
                    w = min(512, L - kc * 512)
                    bk_i = kc % 4
                    mm3(bk_i, w, [(qx_hi[:, qt * 128:(qt + 1) * 128],
                                   qx_lo[:, qt * 128:(qt + 1) * 128],
                                   kx_hi[:, kc * 512:kc * 512 + w],
                                   kx_lo[:, kc * 512:kc * 512 + w])],
                        (tQxh, tQxl, tKxh, tKxl))
                    last = (kc == nkc - 1)
                    wm = w - 128 if last else w
                    if wm > 0:
                        p.group("scalar", [lambda e, o=scb[:, kc * 512:
                                kc * 512 + wm], i=pbank[bk_i][:, 0:wm]:
                                e.activation(o, i, AF.Identity, bias=0.0,
                                             scale=1.0)],
                                reads=(tPB[bk_i],), writes=(tScb[kc],),
                                inc=SE, accum=True)
                    if last:
                        p.group("vector", [lambda e, o=scb[:, L - 128:L],
                                i=pbank[bk_i][:, w - 128:w], m=maskt[:]:
                                e.tensor_tensor(o, i, m, OP.add)],
                                reads=(tPB[bk_i], tMk), writes=(tScb[kc],),
                                inc=VE, accum=True)
                    p.group("vector", [lambda e, o=cm[sl][:, kc:kc + 1],
                            i=scb[:, kc * 512:kc * 512 + w]:
                            e.reduce_max(o, i, axis=AX.X, negate=True)],
                            reads=(tScb[kc],), writes=(tCm[sl],), inc=VE,
                            accum=True)

            def emit_softmax_part(sl, qt, pr_):
                L = (qt + 1) * 128
                nkc = (L + 511) // 512
                scb, tScb = SC(sl, pr_)
                if _os0.environ.get("NOCHUNK"):
                    p.group("vector", [lambda e, o=nm[sl][:], i=scb[:, 0:L]:
                            e.reduce_max(o, i, axis=AX.X, negate=True)],
                            reads=(tScb,), writes=(tNm[sl],), inc=VE)
                else:
                    p.group("vector", [lambda e, o=nm[sl][:], i=cm[sl][:]:
                            e.tensor_reduce(o, i, axis=AX.X, op=OP.min)],
                            reads=(tCm[sl],), writes=(tNm[sl],), inc=VE)
                p.group("scalar", [lambda e, o=nms[sl][:], i=nm[sl][:]:
                        e.activation(o, i, AF.Identity, bias=0.0, scale=INVSQ)],
                        reads=(tNm[sl],), writes=(tNms[sl],), inc=SE)
                p.group("scalar", [lambda e, o=p_sb[sl][:, 0:L],
                        i=scb[:, 0:L], b=nms[sl][:, 0:1], a=ssum[sl][:]:
                        e.activation(o, i, AF.Exp, bias=b, scale=INVSQ,
                                     accum_out=a)],
                        reads=tuple(tScb[:nkc]) + (tNms[sl],),
                        writes=(tP[sl], tSs[sl]), inc=SE)
                p.group("vector", [lambda e, o=rcp[sl][:], i=ssum[sl][:]:
                        e.reciprocal(o, i)],
                        reads=(tSs[sl],), writes=(tRcp[sl],), inc=VE)
                p.group("vector", [lambda e, o=scb[:, 0:L],
                        i=p_sb[sl][:, 0:L], r=rcp[sl][:, 0:1]:
                        e.tensor_scalar_mul(o, i, r)],
                        reads=(tP[sl], tRcp[sl]), writes=tuple(tScb[:nkc]),
                        inc=VE)

            def emit_pt_part(sl, qt, pr_):
                nkb = qt + 1
                scb, tScb = SC(sl, pr_)
                for kbg in range((nkb + 3) // 4):
                    n4 = min(4, nkb - kbg * 4)
                    bk_i = 4 + (kbg % 2)
                    transposes(bk_i, [scb[:, (kbg * 4 + k) * 128:
                                               (kbg * 4 + k + 1) * 128]
                                      for k in range(n4)], (tScb[kbg],))
                    split_psum(bk_i, n4 * 128,
                               pt_hi[sl][:, kbg * 512:kbg * 512 + n4 * 128],
                               pt_lo[sl][:, kbg * 512:kbg * 512 + n4 * 128],
                               tPth[sl], tPtl[sl], accum=True)

            def emit_pv_part(sl, qt):
                nkb = qt + 1
                bk_i = 6 + sl
                pairs = [(vt_hi[:, kb * 128:(kb + 1) * 128],
                          vt_lo[:, kb * 128:(kb + 1) * 128],
                          pt_hi[sl][:, kb * 128:(kb + 1) * 128],
                          pt_lo[sl][:, kb * 128:(kb + 1) * 128])
                         for kb in range(nkb)]
                mm3(bk_i, 128, pairs, (tVth, tVtl, tPth[sl], tPtl[sl]))
                split_psum(bk_i, 128, aoh[sl][:], aol[sl][:], tAoh[sl], tAol[sl])
                dma("gpsimd", AO[sl], aod[0, h, :, qt * 128:(qt + 1) * 128],
                    aoh[sl][:], reads=(tAoh[sl],))
                dma("gpsimd", AO[sl], aod[1, h, :, qt * 128:(qt + 1) * 128],
                    aol[sl][:], reads=(tAol[sl],))
                tAOD.ready.append((AO[sl], AO[sl].v))

            if prologue[0]:
                emit_scores_part(0, pr, pr)
                prologue[0] = False
            emit_scores_part(1, NQT - 1 - pr, pr)
            emit_softmax_part(0, pr, pr)
            emit_pt_part(0, pr, pr)
            emit_softmax_part(1, NQT - 1 - pr, pr)
            emit_pv_part(0, pr)
            if pr + 1 < NQT // 2:
                emit_scores_part(0, pr + 1, pr + 1)
            emit_pt_part(1, NQT - 1 - pr, pr)
            emit_pv_part(1, NQT - 1 - pr)
            for _ in range(prep_per_pair):
                emit_prep_unit()
    while prep_pos[0] < len(prep_units):
        emit_prep_unit()

    p.barrier()
    ctxB.close()

    ctxB2 = ExitStack()
    oT_hi, tOTh = mksb(ctxB2, "oT_hi", [128, HPC * D], BF16)
    oT_lo, tOTl = mksb(ctxB2, "oT_lo", [128, HPC * D], BF16)
    won = []; tWon = []
    for i in range(2):
        a_, ta_ = mksb(ctxB2, f"won{i}", [128, D]); won.append(a_); tWon.append(ta_)
    aot_hi = []; aot_lo = []; tAth = []; tAtl = []
    for i in range(2):
        a_, ta_ = mksb(ctxB2, f"aoth{i}", [128, HPC * 128], BF16)
        aot_hi.append(a_); tAth.append(ta_)
        b_, tb_ = mksb(ctxB2, f"aotl{i}", [128, HPC * 128], BF16)
        aot_lo.append(b_); tAtl.append(tb_)
    pos_ = []; tPos = []
    for i in range(2):
        a_, ta_ = mksb(ctxB2, f"pos{i}", [128, D]); pos_.append(a_); tPos.append(ta_)

    for hc in range(HPC):
        buf = hc % 2
        dma("sync", WN[buf], won[buf][:], wo[hc], writes=(tWon[buf],))
        split_sbuf(won[buf][:], tWon[buf], oT_hi[:, hc * D:(hc + 1) * D],
                   oT_lo[:, hc * D:(hc + 1) * D], tOTh, tOTl, accum=True)

    if _os0.environ.get("DBG_OT"):
        dbgot0 = nc.dram_tensor("dbgot0", [2, 128, HPC * D], BF16,
                                kind="ExternalOutput").ap()
        dma("gpsimd", OU, dbgot0[0], oT_hi[:], reads=(tOTh,))
        dma("gpsimd", OU, dbgot0[1], oT_lo[:], reads=(tOTl,))

    NOG = D // 512
    tCCIN = _T()
    for tt in range(S // 128):
        buf = tt % 2
        for hc in range(HPC):
            dma("sync", AT[buf], aot_hi[buf][:, hc * 128:(hc + 1) * 128],
                aod[0, hc, :, tt * 128:(tt + 1) * 128], reads=(tAOD,),
                writes=(tAth[buf],) if hc == HPC - 1 else (),
                extra=tAth[buf].ready + tAth[buf].users if hc == 0 else ())
            dma("sync", AT[buf], aot_lo[buf][:, hc * 128:(hc + 1) * 128],
                aod[1, hc, :, tt * 128:(tt + 1) * 128], reads=(tAOD,),
                writes=(tAtl[buf],) if hc == HPC - 1 else (),
                extra=tAtl[buf].ready + tAtl[buf].users if hc == 0 else ())
        rb = (tt * 128) // OT
        off = (tt * 128) % OT
        sb_ = tt % 2
        for og in range(NOG):
            bk_i = og % 4
            pairs = []
            for hc in range(HPC):
                pairs.append((aot_hi[buf][:, hc * 128:(hc + 1) * 128],
                              aot_lo[buf][:, hc * 128:(hc + 1) * 128],
                              oT_hi[:, hc * D + og * 512:hc * D + (og + 1) * 512],
                              oT_lo[:, hc * D + og * 512:hc * D + (og + 1) * 512]))
            mm3(bk_i, 512, pairs, (tOTh, tOTl, tAth[buf], tAtl[buf]))
            p.group("scalar", [lambda e, o=pos_[sb_][:, og * 512:(og + 1) * 512],
                    i=pbank[bk_i][:, 0:512]:
                    e.activation(o, i, AF.Identity, bias=0.0, scale=1.0)],
                    reads=(tPB[bk_i],), writes=(tPos[sb_],), inc=SE, accum=True)
        dma("gpsimd", CI[sb_], cc_in.ap()[rb, off:off + 128, :], pos_[sb_][:],
            reads=(tPos[sb_],))
        tCCIN.ready.append((CI[sb_], CI[sb_].v))

    if _os0.environ.get("DBG_OT"):
        dbgot1 = nc.dram_tensor("dbgot1", [2, 128, HPC * D], BF16,
                                kind="ExternalOutput").ap()
        dma("gpsimd", OU, dbgot1[0], oT_hi[:], reads=(tOTh, tCCIN))
        dma("gpsimd", OU, dbgot1[1], oT_lo[:], reads=(tOTl, tCCIN))
        dbgwon = nc.dram_tensor("dbgwon", [128, D], F32,
                                kind="ExternalOutput").ap()
        dma("gpsimd", OU, dbgwon, won[0][:], reads=(tWon[0], tCCIN))
        dbgwo = nc.dram_tensor("dbgwo", [128, D], F32,
                               kind="ExternalOutput").ap()
        dma("gpsimd", OU, dbgwo, wo[0], reads=(tCCIN,))
        tCCIN.ready.append((OU, OU.v))

    if _os0.environ.get("DBG_CCIN"):
        dbgci = nc.dram_tensor("dbgci", [G, OT, D], F32, kind="ExternalOutput").ap()
        p.group("gpsimd", [lambda e: e.dma_start(out=dbgci, in_=cc_in.ap())],
                inc=OU, amt=16, reads=(tCCIN,))
        tCCIN.ready.append((OU, OU.v))
    fl_out = nc.dram_tensor("fl_out", [128], F32)
    p.group("gpsimd", [lambda e: e.collective_compute(
        "ReduceScatter", OP.add,
        replica_groups=[[0, 1, 2, 3], [4, 5, 6, 7]],
        ins=[cc_in.ap()], outs=[cc_out.ap()])],
        reads=(tCCIN,), inc=CC)
    p.group("gpsimd", [lambda e: e.collective_compute(
        "AllReduce", OP.add,
        replica_groups=[[0, 1, 2, 3, 4, 5, 6, 7]],
        ins=[cc_out.ap().rearrange("a b -> (a b)")[0:128]],
        outs=[fl_out.ap()])],
        extra_waits=[(CC, CC.v)], inc=CC)
    tCCO = _T()
    tCCO.ready = [(CC, CC.v)]

    p.barrier(exclude=("CC",))
    ctxB2.close()

    ctxC = ExitStack()
    xq_hi, tXqh = mksb(ctxC, "xq_hi", [128, IC * TQ], BF16)
    xq_lo, tXql = mksb(ctxC, "xq_lo", [128, IC * TQ], BF16)
    xq_tm = []; tXqtm = []
    for i in range(1):
        a_, ta_ = mksb(ctxC, f"xqtm{i}", [128, D]); xq_tm.append(a_); tXqtm.append(ta_)
    xo_sub = []; tXos = []
    for i in range(2):
        a_, ta_ = mksb(ctxC, f"xos{i}", [128, D]); xo_sub.append(a_); tXos.append(ta_)
    h_hi, tHh = mksb(ctxC, "h_hi", [128, FH * TQ], BF16)
    h_lo, tHl = mksb(ctxC, "h_lo", [128, FH * TQ], BF16)
    hf_sb, tHf = mksb(ctxC, "hf", [128, TQ])
    w1b_hi = []; w1b_lo = []; tW1h = []; tW1l = []
    for i in range(2):
        a_, ta_ = mksb(ctxC, f"w1bh{i}", [128, IC * 128], BF16); w1b_hi.append(a_); tW1h.append(ta_)
        b_, tb_ = mksb(ctxC, f"w1bl{i}", [128, IC * 128], BF16); w1b_lo.append(b_); tW1l.append(tb_)
    w2b_hi = []; w2b_lo = []; tW2h = []; tW2l = []
    for i in range(2):
        a_, ta_ = mksb(ctxC, f"w2bh{i}", [128, D], BF16); w2b_hi.append(a_); tW2h.append(ta_)
        b_, tb_ = mksb(ctxC, f"w2bl{i}", [128, D], BF16); w2b_lo.append(b_); tW2l.append(tb_)
    g_tm = []; tGtm = []
    s_t = []; tSt = []
    outst = []; tOut = []
    for i in range(2):
        a_, ta_ = mksb(ctxC, f"gtm{i}", [128, D]); g_tm.append(a_); tGtm.append(ta_)
        b_, tb_ = mksb(ctxC, f"st{i}", [128, D]); s_t.append(b_); tSt.append(tb_)
        d_, td_ = mksb(ctxC, f"outst{i}", [128, D]); outst.append(d_); tOut.append(td_)
    lnC = ln_pipeline(ctxC, "lnC", scr_pair=(xq_tm[0], tXqtm[0]))

    ccw = [(CC, CC.v)]
    NQV = OT // TQ
    NSQ = TQ // 128
    for qv in range(NQV):
        xqr_hi = xq_hi[:].rearrange("q (i c) -> q i c", c=TQ)
        xqr_lo = xq_lo[:].rearrange("q (i c) -> q i c", c=TQ)
        for a2 in range(NSQ):
            buf = a2 % 2
            g = qv * TQ + a2 * 128
            dma("sync", XS[buf], xo_sub[buf][:], xown[g:g + 128, :], writes=(tXos[buf],),
                extra=ccw)
            lnC(xo_sub[buf][:], tXos[buf], xq_tm[0][:], tXqtm[0])
            for icg in range(IC // 4):
                bk_i = 6 + (icg % 2)
                transposes(bk_i, [xq_tm[0][:, (icg * 4 + k) * 128:
                                             (icg * 4 + k + 1) * 128]
                                  for k in range(4)], (tXqtm[0],))
                hi_ap = xqr_hi[:, icg * 4:(icg + 1) * 4, a2 * 128:(a2 + 1) * 128]
                lo_ap = xqr_lo[:, icg * 4:(icg + 1) * 4, a2 * 128:(a2 + 1) * 128]
                split_psum(bk_i, 512, hi_ap, lo_ap, tXqh, tXql, accum=True,
                           three_d=True)
        for hc in range(FH):
            buf = hc % 2
            dma("sync", WP[buf], w1b_hi[buf][:], w1td[0, hc], reads=(tW1T,),
                writes=(tW1h[buf],), extra=ccw)
            dma("sync", WP[buf], w1b_lo[buf][:], w1td[1, hc], reads=(tW1T,),
                writes=(tW1l[buf],), extra=ccw)
            bk_i = (0, 1, 4, 5)[hc % 4]
            pairs = []
            for ic in range(IC):
                pairs.append((w1b_hi[buf][:, ic * 128:(ic + 1) * 128],
                              w1b_lo[buf][:, ic * 128:(ic + 1) * 128],
                              xq_hi[:, ic * TQ:(ic + 1) * TQ],
                              xq_lo[:, ic * TQ:(ic + 1) * TQ]))
            mm3(bk_i, TQ, pairs, (tW1h[buf], tW1l[buf], tXqh, tXql))
            p.group("scalar", [lambda e, o=hf_sb[:], i=pbank[bk_i][:, 0:TQ],
                    b=b1_sb[:, hc:hc + 1]: e.activation(o, i, AF.Gelu, bias=b,
                                                        scale=1.0)],
                    reads=(tPB[bk_i], tB1), writes=(tHf,), inc=SE)
            split_sbuf(hf_sb[:], tHf, h_hi[:, hc * TQ:(hc + 1) * TQ],
                       h_lo[:, hc * TQ:(hc + 1) * TQ], tHh, tHl, accum=True)
        for hc in range(FH):
            buf = hc % 2
            dma("sync", W2S[buf], w2b_hi[buf][:], w2td[0, hc], reads=(tW2T,),
                writes=(tW2h[buf],), extra=ccw)
            dma("sync", W2S[buf], w2b_lo[buf][:], w2td[1, hc], reads=(tW2T,),
                writes=(tW2l[buf],), extra=ccw)
            fns = []
            rd = (tW2h[buf], tW2l[buf], tHh, tHl)
            first = (hc == 0)
            last = (hc == FH - 1)
            for t2 in range(NSQ):
                for og in range(D // 512):
                    bk_i = og * NSQ + t2
                    lh = h_hi[:, hc * TQ + t2 * 128:hc * TQ + (t2 + 1) * 128]
                    ll = h_lo[:, hc * TQ + t2 * 128:hc * TQ + (t2 + 1) * 128]
                    rh = w2b_hi[buf][:, og * 512:(og + 1) * 512]
                    rl = w2b_lo[buf][:, og * 512:(og + 1) * 512]
                    for pi, (a_, b_) in enumerate(((lh, rh), (lh, rl),
                                                   (ll, rh))):
                        fns.append(lambda e, o=pbank[bk_i][:, 0:512], x=a_,
                                   y=b_, s=(first and pi == 0),
                                   t=(last and pi == 2):
                                   e.matmul(o, x, y, start=s, stop=t))
            if first:
                ws = []
                for bi in range(NSQ * (D // 512)):
                    ws += tPB[bi].ready + tPB[bi].users
                p.group("tensor", fns, reads=rd, extra_waits=ws, inc=MM)
            elif last:
                p.group("tensor", fns, reads=rd, inc=MM)
                for bi in range(NSQ * (D // 512)):
                    tPB[bi].ready = [(MM, MM.v)]
                    tPB[bi].users = []
            else:
                p.group("tensor", fns, reads=rd, inc=MM)
        for t2 in range(NSQ):
            for og in range(D // 512):
                bk_i = og * NSQ + t2
                p.group("vector", [lambda e, o=s_t[t2 % 2][:, og * 512:(og + 1) * 512],
                        i=pbank[bk_i][:, 0:512], b=b2b_sb[:, og * 512:(og + 1) * 512]:
                        e.tensor_tensor(o, i, b, OP.add)],
                        reads=(tPB[bk_i], tB2b), writes=(tSt[t2 % 2],), inc=VE,
                        accum=True)
            p.group("scalar", [lambda e, o=g_tm[t2][:], i=s_t[t2 % 2][:]:
                    e.activation(o, i, AF.Silu, bias=0.0, scale=1.0)],
                    reads=(tSt[t2 % 2],), writes=(tGtm[t2],), inc=SE)
        for t2 in range(NSQ):
            buf = t2 % 2
            g0 = qv * TQ + t2 * 128
            dma("sync", SCS[buf], s_t[buf][:], cc_out.ap()[g0:g0 + 128, :],
                reads=(tCCO,), writes=(tSt[buf],), extra=ccw)
            p.group("vector", [lambda e, o=outst[buf][:], i=s_t[buf][:],
                    b=bo_sb[:]: e.tensor_tensor(o, i, b, OP.add)],
                    reads=(tSt[buf], tBo), writes=(tOut[buf],), inc=VE)
            p.group("vector", [lambda e, o=s_t[buf][:], i=outst[buf][:],
                    g_=g_tm[t2][:]: e.tensor_tensor(o, i, g_, OP.mult)],
                    reads=(tOut[buf], tGtm[t2]), writes=(tSt[buf],), inc=VE)
            p.group("vector", [lambda e, o=outst[buf][:], i=s_t[buf][:],
                    x=xo_sub[t2 % 2][:]: e.tensor_tensor(o, i, x, OP.add)],
                    reads=(tSt[buf], tXos[t2 % 2]), writes=(tOut[buf],), inc=VE)
            dma("gpsimd", OD[buf], outd[g0:g0 + 128, :], outst[buf][:],
                reads=(tOut[buf],))

    if _os0.environ.get("DBG_SG"):
        dbgs = nc.dram_tensor("dbgs", [OT, D], F32, kind="ExternalOutput").ap()
        allw = [(s_, s_.v) for s_ in p.sems if s_.v > 0]
        p.group("gpsimd", [lambda e: e.dma_start(out=dbgs, in_=cc_out.ap())],
                inc=OU, amt=16, extra_waits=allw)
    p.final_waits = [(s_.name, s_.v) for s_ in p.sems if s_.v > 0]
    ctxC.close()

    hsem = {}
    for s in p.sems:
        hsem[s.name] = gctx.enter_context(nc.semaphore(s.name))
    _emit(nc, p, hsem)
    gctx.close()
    return nc


def _in_maps(cfg, x, q_w, k_w, v_w, o_w, q_b, k_b, v_b, o_b, w1, b1, w2, b2):
    S, D, F, HPC = cfg["S"], cfg["D"], cfg["F"], cfg["HPC"]
    OT = S // 4
    IC = D // 128
    FH = F // 128
    ident = np.eye(128, dtype=np.float32)
    ii, jj = np.indices((128, 128))
    mask = np.where(jj > ii, np.float32(NEG), np.float32(0.0))
    maps = []
    c32 = np.ascontiguousarray

    def tile_oc(w):
        O, I = w.shape
        return c32(w.reshape(O // 128, 128, I // 128, 128).transpose(0, 3, 2, 1))

    w1t_host = tile_oc(w1)
    D_ = w2.shape[0]
    F_ = w2.shape[1]
    w2t_host = c32(w2.T).reshape(F_ // 128, 128, D_)
    for c in range(8):
        b = c // 4
        j = c % 4
        H0 = j * HPC * 128
        HD = HPC * 128
        maps.append({
            "xb": c32(x[b]),
            "xown": c32(x[b, j * OT:(j + 1) * OT]),
            "wq": tile_oc(q_w[H0:H0 + HD]),
            "wk": tile_oc(k_w[H0:H0 + HD]),
            "wv": tile_oc(v_w[H0:H0 + HD]),
            "wo": c32(o_w[:, H0:H0 + HD].T).reshape(HPC, 128, D_),
            "w1": w1t_host,
            "w2": w2t_host,
            "bqs": c32(q_b[H0:H0 + HD].reshape(HPC, 128).T),
            "bks": c32(k_b[H0:H0 + HD].reshape(HPC, 128).T),
            "bvs": c32(v_b[H0:H0 + HD].reshape(HPC, 128).T),
            "bos": c32(np.broadcast_to(o_b, (128, D_))),
            "b1s": c32(b1.reshape(FH, 128).T),
            "b2s": c32(b2.reshape(IC, 128).T),
            "b2bc": c32(np.broadcast_to(b2, (128, D_))),
            "identd": ident,
            "maskd": mask,
        })
    return maps


def run(cfg, x, q_w, k_w, v_w, o_w, q_b, k_b, v_b, o_b, w1, b1, w2, b2):
    nc = build(cfg)
    maps = _in_maps(cfg, x, q_w, k_w, v_w, o_w, q_b, k_b, v_b, o_b, w1, b1, w2, b2)
    res = run_bass_kernel_spmd(nc, maps, core_ids=list(range(8)))
    S, D = cfg["S"], cfg["D"]
    OT = S // 4
    out = np.empty((2, S, D), np.float32)
    for c in range(8):
        out[c // 4, (c % 4) * OT:(c % 4 + 1) * OT] = res.results[c]["out"]
    return out


def kernel(L0_attn_hidden_states, L0_attn_q_w, L0_attn_k_w, L0_attn_v_w,
           L0_attn_o_w, L0_attn_q_b, L0_attn_k_b, L0_attn_v_b, L0_attn_o_b,
           L0_ffn_proj1, L0_ffn_proj1_bias, L0_ffn_proj2, L0_ffn_proj2_bias,
           L0_ffn_gate_w):
    cfg = dict(S=4096, D=2048, F=8192, HPC=4, TB=512, TQ=256)
    return run(cfg, np.asarray(L0_attn_hidden_states),
               np.asarray(L0_attn_q_w), np.asarray(L0_attn_k_w),
               np.asarray(L0_attn_v_w), np.asarray(L0_attn_o_w),
               np.asarray(L0_attn_q_b), np.asarray(L0_attn_k_b),
               np.asarray(L0_attn_v_b), np.asarray(L0_attn_o_b),
               np.asarray(L0_ffn_proj1), np.asarray(L0_ffn_proj1_bias),
               np.asarray(L0_ffn_proj2), np.asarray(L0_ffn_proj2_bias))
